# version 18
# baseline (speedup 1.0000x reference)
import math
import numpy as np
import ml_dtypes
from contextlib import ExitStack

import concourse.bass as bass
import concourse.mybir as mybir
from concourse.bass_utils import run_bass_kernel_spmd

F32 = mybir.dt.float32
BF16 = mybir.dt.bfloat16
AF = mybir.ActivationFunctionType
ALU = mybir.AluOpType
BF = ml_dtypes.bfloat16


class Sem:
    __slots__ = ("h", "n", "name")

    def __init__(self, h, name):
        self.h = h
        self.n = 0
        self.name = name

    def bump(self, k):
        self.n += k
        return self.n


class Sched:
    def __init__(self):
        self.ops = []

    def add(self, eng, fn, waits=(), incs=()):
        self.ops.append((eng, tuple(w for w in waits if w and w[1] > 0), fn,
                         tuple(incs)))

    def emit(self, eng_name, eng):
        for e, waits, fn, incs in self.ops:
            if e != eng_name:
                continue
            for s, v in waits:
                eng.wait_ge(s.h, v)
            r = fn(eng)
            for s, k in incs:
                assert r is not None
                r = r.then_inc(s.h, k)


def build_mk(cfg):
    D = cfg["D"]; TL = cfg["TL"]; TC = cfg["TC"]
    E = cfg.get("E", 8); DK = cfg.get("DK", 32); R = cfg.get("R", 16)
    C = D // 128
    NH = D // 128
    NTC = TL // TC
    EDK = E * DK
    EOC = EDK // 128
    ER = E * R
    assert EDK == 256 and ER == 128
    assert NH == 32
    TB = cfg.get("TB", 256)
    NB = TL // TB
    NGRP = TB // 16
    eps = 1e-5
    inv_d = 1.0 / D
    sc_mix = 1.0 / math.sqrt(128.0)
    sc_e = 1.0 / math.sqrt(DK)
    PSW = 512

    nc = bass.Bass("TRN2", target_bir_lowering=False,
                   detect_race_conditions=cfg.get("races", True))

    def din(name, shape, dt=F32):
        return nc.dram_tensor(name, shape, dt, kind="ExternalInput")

    xT = din("xT", [D, TL])
    WT = din("WT", [D, D], BF16)
    WqT = din("WqT", [D, EDK], BF16)
    WkT = din("WkT", [D, EDK], BF16)
    AT_d = din("AT", [D, ER], BF16)
    BmT_d = din("BmT", [ER, D], BF16)
    b_f = din("b_f", [1, D], BF16)
    wqT = din("wqT", [D, D], BF16)
    wkT = din("wkT", [D, D], BF16)
    wvT = din("wvT", [D, D], BF16)
    bq_f = din("bq_f", [1, D], BF16)
    bk_f = din("bk_f", [1, D], BF16)
    bv_f = din("bv_f", [1, D], BF16)
    onescol_d = din("onescol", [128, 1], BF16)
    onesrow_d = din("onesrow", [1, 512], BF16)
    ones128f_d = din("ones128f", [1, 128], F32)
    seg_d = din("seg", [128, 16], BF16)
    seg16_d = din("seg16", [8, 128], BF16)
    ident_d = din("ident", [128, 128], BF16)
    identf_d = din("identf", [128, 128])

    y1T = nc.dram_tensor("y1T", [D, TL], F32)
    y2T = nc.dram_tensor("y2T", [D, TL], F32)
    qTs = nc.dram_tensor("qTs", [D, TL], BF16)
    kTs = nc.dram_tensor("kTs", [D, TL], BF16)
    v_tm = nc.dram_tensor("v_tm", [TL, D], BF16)
    OUT = nc.dram_tensor("OUT", [D, TL], F32, kind="ExternalOutput")

    SYM = {}

    with ExitStack() as top:
        def sem(name):
            return Sem(top.enter_context(nc.semaphore(name)), name)

        def sems(name, k):
            return [sem(f"{name}{j}") for j in range(k)]

        sCst = sem("sCst")
        sXi = sems("sXi", 4)
        sXr = sems("sXr", 4)
        sYo = sems("sYo", 4)
        sW = sems("sW", 2)
        sMisc = sem("sMisc")
        sBlk = sems("sBlk", 2)
        sY1 = sems("sY1", 4)
        sY2 = sems("sY2", 4)
        sQs = sems("sQs", 4)
        sPE = sem("sPE"); sDV = sem("sDV"); sAC = sem("sAC")
        sGP = sem("sGP")

        def sbt(name, shape, dt):
            return top.enter_context(nc.sbuf_tensor(name, shape, dt))

        onescol = sbt("onescol_s", [128, 1], BF16)
        onesrow = sbt("onesrow_s", [1, 512], BF16)
        ones128f = sbt("ones128f_s", [1, 128], F32)
        seg_sb = sbt("seg_s", [128, 16], BF16)
        seg16_sb = sbt("seg16_s", [8, 128], BF16)
        identsb = sbt("ident_s", [128, 128], BF16)
        identfsb = sbt("identf_s", [128, 128], F32)
        brsb = sbt("br_s", [128, TL], BF16)
        bnsb = sbt("bn_s", [128, TL], BF16)

        def run_block(sched, name):
            with nc.Block(name) as blk:
                blk.sync(lambda e: sched.emit("sync", e))
                blk.tensor(lambda e: sched.emit("tensor", e))
                blk.vector(lambda e: sched.emit("vector", e))
                blk.scalar(lambda e: sched.emit("scalar", e))
                blk.gpsimd(lambda e: sched.emit("gpsimd", e))

        def addseq(sched, eng, sm, fns, waits=(), inc_final=True):
            base = sm.n
            nsub = len(fns)

            def op(e, fns=tuple(fns), base=base, nsub=nsub):
                last = None
                for j, f in enumerate(fns):
                    if j:
                        e.wait_ge(sm.h, base + j)
                    last = f(e)
                    if j < nsub - 1:
                        last.then_inc(sm.h, 1)
                return last
            sm.bump(nsub - 1)
            incs = [(sm, 1)] if inc_final else []
            sched.add(eng, op, waits, incs)
            if inc_final:
                return sm.bump(1)
            return sm.n

        s0 = Sched()
        for ap_d, ap_s in (
            (onescol_d.ap(), onescol[:, :]),
            (onesrow_d.ap(), onesrow[:, :]),
            (ones128f_d.ap(), ones128f[:, :]),
            (seg_d.ap(), seg_sb[:, :]),
            (seg16_d.ap(), seg16_sb[:, :]),
            (ident_d.ap(), identsb[:, :]),
            (identf_d.ap(), identfsb[:, :]),
        ):
            s0.add("sync",
                   (lambda e, a=ap_s, b=ap_d: e.dma_start(out=a, in_=b)),
                   incs=[(sCst, 16)])
            sCst.bump(16)
        s0.add("sync", lambda e, v=sCst.n: e.wait_ge(sCst.h, v) and None)
        v_cst = sCst.n
        run_block(s0, "consts")

        def ln_pass(sched, stack, src, tag, lnbuf):
            xin = [stack.enter_context(nc.sbuf_tensor(f"{tag}_xin{j}", [128, TL], F32))
                   for j in range(4)]
            xsq = [stack.enter_context(nc.sbuf_tensor(f"{tag}_xsq{j}", [128, TL], BF16))
                   for j in range(2)]
            stats = stack.enter_context(nc.sbuf_tensor(f"{tag}_stats", [1, 3 * TL], F32))
            mean = stats[0:1, 0:TL]
            var = stats[0:1, TL:2 * TL]
            rstd = stats[0:1, 2 * TL:3 * TL]
            negmr = var

            cast_done = [0, 0, 0, 0]; sq_done = [0, 0, 0, 0]; pe_done = [0, 0]
            with nc.psum_tensor(f"{tag}_psm", [1, NTC * PSW], F32) as ps_m, \
                 nc.psum_tensor(f"{tag}_pss", [1, NTC * PSW], F32) as ps_s:
                for i in range(C):
                    pp = i % 4
                    qq = i % 2
                    waits = ([(sDV, cast_done[pp]), (sAC, sq_done[pp])]
                             if i >= 4 else [(sCst, v_cst)])
                    sched.add("sync" if i % 2 == 0 else "gpsimd",
                              (lambda e, a=xin[pp], s=src, i=i:
                               e.dma_start(out=a[:, :],
                                           in_=s.ap()[i * 128:(i + 1) * 128, :])),
                              waits=waits, incs=[(sXi[pp], 16)])
                    v_ld = sXi[pp].bump(16)
                    sched.add("vector",
                              (lambda e, a=xin[pp], i=i:
                               e.tensor_copy(lnbuf[:, i * TL:(i + 1) * TL], a[:, :])),
                              waits=[(sXi[pp], v_ld)], incs=[(sDV, 1)])
                    cast_done[pp] = sDV.bump(1)
                    wsq = [(sXi[pp], v_ld)] + ([(sPE, pe_done[qq])] if i >= 2 else [])
                    sched.add("scalar",
                              (lambda e, a=xsq[qq], b=xin[pp]:
                               e.activation(a[:, :], b[:, :], AF.Square)),
                              waits=wsq, incs=[(sAC, 1)])
                    sq_done[pp] = sAC.bump(1)

                    def statmm(e, i=i, qq=qq):
                        first = (i == 0); last = (i == C - 1)
                        r = None
                        for t in range(NTC):
                            r = e.matmul(ps_m[0:1, t * PSW:t * PSW + TC], onescol[:, :],
                                         lnbuf[:, i * TL + t * TC: i * TL + (t + 1) * TC],
                                         start=first, stop=last)
                        for t in range(NTC):
                            r = e.matmul(ps_s[0:1, t * PSW:t * PSW + TC], onescol[:, :],
                                         xsq[qq][:, t * TC:(t + 1) * TC],
                                         start=first, stop=last)
                        return r
                    sched.add("tensor", statmm,
                              waits=[(sDV, cast_done[pp]), (sAC, sq_done[pp])],
                              incs=[(sPE, 1)])
                    pe_done[qq] = sPE.bump(1)

                v_stats_pe = sPE.n
                fns = []
                for t in range(NTC):
                    fns.append(lambda e, t=t: e.tensor_copy(
                        mean[0:1, t * TC:(t + 1) * TC],
                        ps_m[0:1, t * PSW:t * PSW + TC]))
                    fns.append(lambda e, t=t: e.tensor_copy(
                        var[0:1, t * TC:(t + 1) * TC],
                        ps_s[0:1, t * PSW:t * PSW + TC]))
                fns += [
                    lambda e: e.tensor_scalar_mul(mean, mean, inv_d),
                    lambda e: e.tensor_scalar(var, var, inv_d, eps, ALU.mult, ALU.add),
                    lambda e: e.scalar_tensor_tensor(rstd, mean, -1.0, mean,
                                                     ALU.mult, ALU.mult),
                    lambda e: e.tensor_add(var, var, rstd),
                ]
                v1 = addseq(sched, "vector", sDV, fns, waits=[(sPE, v_stats_pe)])

            sched.add("scalar",
                      (lambda e: e.activation(rstd, var, AF.Sqrt)),
                      waits=[(sDV, v1)], incs=[(sAC, 1)])
            v2 = sAC.bump(1)

            v3 = addseq(sched, "vector", sDV, [
                lambda e: e.reciprocal(rstd, rstd),
                lambda e: e.scalar_tensor_tensor(negmr, mean, -1.0, rstd,
                                                 ALU.mult, ALU.mult),
            ], waits=[(sAC, v2)])

            with nc.psum_tensor(f"{tag}_psr", [128, max(TL, PSW)], F32) as ps_r, \
                 nc.psum_tensor(f"{tag}_psn", [128, max(TL, PSW)], F32) as ps_n:
                def bcasts(e):
                    r = None
                    for t in range(0, TL, 512):
                        w = min(512, TL - t)
                        r = e.matmul(ps_r[:, t:t + w], ones128f[0:1, :],
                                     rstd[0:1, t:t + w], start=True, stop=True)
                        r = e.matmul(ps_n[:, t:t + w], ones128f[0:1, :],
                                     negmr[0:1, t:t + w], start=True, stop=True)
                    return r
                sched.add("tensor", bcasts, waits=[(sDV, v3)], incs=[(sPE, 1)])
                v4 = sPE.bump(1)
                sched.add("vector",
                          (lambda e: e.tensor_copy(brsb[:, :], ps_r[:, 0:TL])),
                          waits=[(sPE, v4)], incs=[(sDV, 1)])
                sDV.bump(1)
                sched.add("vector",
                          (lambda e: e.tensor_copy(bnsb[:, :], ps_n[:, 0:TL])),
                          waits=[(sPE, v4)], incs=[(sDV, 1)])
                v_bc = sDV.bump(1)
                SYM[f"{tag}_ln_prenorm"] = v_bc

                norm_done = []
                for i in range(C):
                    sl = lnbuf[:, i * TL:(i + 1) * TL]
                    eng = "vector" if (i % 3) < 2 else "gpsimd"
                    sm = sDV if eng == "vector" else sGP
                    v = addseq(sched, eng, sm, [
                        (lambda e, sl=sl: e.tensor_mul(sl, sl, brsb[:, :])),
                        (lambda e, sl=sl: e.tensor_add(sl, sl, bnsb[:, :])),
                    ], waits=[(sDV, v_bc)])
                    norm_done.append((sm, v))
            SYM[f"{tag}_norm"] = norm_done
            SYM[f"{tag}_ln_ready"] = sDV.n
            SYM[f"{tag}_ln_ready_gp"] = sGP.n

        def adapters(sched, stack, tag, lnbuf, gTsb):
            ln_rdy = SYM[f"{tag}_ln_ready"]
            ln_rdy_gp = SYM[f"{tag}_ln_ready_gp"]
            norm = SYM[f"{tag}_norm"]
            wq_sb = stack.enter_context(nc.sbuf_tensor(f"{tag}_wq", [128, C * EDK], BF16))
            wk_sb = stack.enter_context(nc.sbuf_tensor(f"{tag}_wk", [128, C * EDK], BF16))
            at_sb = stack.enter_context(nc.sbuf_tensor(f"{tag}_at", [128, C * 128], BF16))
            qf_sb = stack.enter_context(nc.sbuf_tensor(f"{tag}_qf", [128, TL], BF16))
            kf_sb = stack.enter_context(nc.sbuf_tensor(f"{tag}_kf", [128, TL], BF16))
            expe = stack.enter_context(nc.sbuf_tensor(f"{tag}_expe", [8, TL], BF16))
            recipe = stack.enter_context(nc.sbuf_tensor(f"{tag}_recipe", [1, TL], F32))
            we_sb = expe
            hT_sb = stack.enter_context(nc.sbuf_tensor(f"{tag}_hT", [128, TL], BF16))

            v_pre = SYM[f"{tag}_ln_prenorm"]
            for dsrc, dst, q in ((WqT, wq_sb, "gpsimd"), (WkT, wk_sb, "sync"),
                                 (AT_d, at_sb, "scalar")):
                w_o = dst.shape[1] // C
                sched.add(q,
                          (lambda e, a=dst, b=dsrc, w_o=w_o:
                           e.dma_start(out=a[:, :].rearrange("p (c o) -> p c o", o=w_o),
                                       in_=b.ap().rearrange("(c p) o -> p c o", p=128))),
                          waits=[(sDV, v_pre)], incs=[(sMisc, 16)])
                sMisc.bump(16)
            v_w = sMisc.n

            with ExitStack() as st3:
                ps_se = [st3.enter_context(nc.psum_tensor(f"{tag}_se{t}", [8, PSW], F32))
                         for t in range(NTC)]
                ps_qk = [st3.enter_context(nc.psum_tensor(f"{tag}_qk{j}", [128, PSW], F32))
                         for j in range(2)]

                ac_evac = 0
                v_prod = 0
                for oc in range(EOC):
                    for wsb, osb in ((wq_sb, qf_sb), (wk_sb, kf_sb)):
                        for t in range(NTC):
                            pp = t % 2

                            fg = (oc == 0)

                            def qkmm(e, oc=oc, wsb=wsb, t=t, pp=pp, fg=fg):
                                r = None
                                for i in range(C):
                                    if fg and i:
                                        nsm, nv = norm[i]
                                        e.wait_ge(nsm.h, nv)
                                    r = e.matmul(
                                        ps_qk[pp][:, 0:TC],
                                        wsb[:, i * EDK + oc * 128: i * EDK + (oc + 1) * 128],
                                        lnbuf[:, i * TL + t * TC: i * TL + (t + 1) * TC],
                                        start=(i == 0), stop=(i == C - 1))
                                return r
                            w = [(sMisc, v_w), norm[0]] if fg else \
                                [(sMisc, v_w), (sDV, ln_rdy), (sGP, ln_rdy_gp)]
                            if ac_evac:
                                w.append((sAC, ac_evac))
                            sched.add("tensor", qkmm, waits=w, incs=[(sPE, 1)])
                            vq = sPE.bump(1)
                            wev = [(sPE, vq)]
                            if v_prod:
                                wev.append((sDV, v_prod))
                            sched.add("scalar",
                                      (lambda e, o=osb, t=t, pp=pp:
                                       e.activation(o[:, t * TC:(t + 1) * TC],
                                                    ps_qk[pp][:, 0:TC], AF.Copy)),
                                      waits=wev, incs=[(sAC, 1)])
                            ac_evac = sAC.bump(1)
                    sched.add("vector",
                              (lambda e: e.tensor_mul(qf_sb[:, :], qf_sb[:, :],
                                                      kf_sb[:, :])),
                              waits=[(sAC, ac_evac)], incs=[(sDV, 1)])
                    v_prod = sDV.bump(1)

                    def segmm(e, oc=oc):
                        r = None
                        for t in range(NTC):
                            r = e.matmul(ps_se[t][0:8, 0:TC],
                                         seg_sb[:, oc * 8:(oc + 1) * 8],
                                         qf_sb[:, t * TC:(t + 1) * TC],
                                         start=(oc == 0), stop=(oc == EOC - 1))
                        return r
                    sched.add("tensor", segmm, waits=[(sDV, v_prod)], incs=[(sPE, 1)])
                    sPE.bump(1)
                v_seg = sPE.n

                def expe_op(e):
                    r = None
                    for t in range(NTC):
                        r = e.activation(expe[0:8, t * TC:(t + 1) * TC],
                                         ps_se[t][0:8, 0:TC], AF.Exp, scale=sc_e)
                    return r
                sched.add("scalar", expe_op, waits=[(sPE, v_seg)], incs=[(sAC, 1)])
                v_expe = sAC.bump(1)

                recip_t = {}
                for t in range(NTC):
                    pp = t % 2
                    w = [(sAC, v_expe), (sAC, ac_evac)]
                    if t >= 2:
                        w.append((sDV, recip_t[t - 2]))
                    sched.add("tensor",
                              (lambda e, t=t, pp=pp:
                               e.matmul(ps_qk[pp][0:1, 0:TC], onescol[0:8, 0:1],
                                        expe[0:8, t * TC:(t + 1) * TC],
                                        start=True, stop=True)),
                              waits=w, incs=[(sPE, 1)])
                    vsm = sPE.bump(1)
                    sched.add("vector",
                              (lambda e, t=t, pp=pp:
                               e.reciprocal(recipe[0:1, t * TC:(t + 1) * TC],
                                            ps_qk[pp][0:1, 0:TC])),
                              waits=[(sPE, vsm)], incs=[(sDV, 1)])
                    recip_t[t] = sDV.bump(1)
                v_recip = sDV.n

                for t in range(NTC):
                    sched.add("tensor",
                              (lambda e, t=t:
                               e.matmul(ps_se[t][0:8, 0:TC], ones128f[0:1, 0:8],
                                        recipe[0:1, t * TC:(t + 1) * TC],
                                        start=True, stop=True)),
                              waits=[(sDV, v_recip), (sAC, v_expe)], incs=[(sPE, 1)])
                    vb8 = sPE.bump(1)
                    sched.add("vector",
                              (lambda e, t=t:
                               e.tensor_mul(we_sb[0:8, t * TC:(t + 1) * TC],
                                            expe[0:8, t * TC:(t + 1) * TC],
                                            ps_se[t][0:8, 0:TC])),
                              waits=[(sPE, vb8)], incs=[(sDV, 1)])
                    sDV.bump(1)
                v_we = sDV.n

                dv_h = 0
                for t in range(NTC):
                    pp = t % 2

                    def hmm(e, t=t, pp=pp):
                        r = None
                        for i in range(C):
                            r = e.matmul(ps_qk[pp][:, 0:TC],
                                         at_sb[:, i * 128:(i + 1) * 128],
                                         lnbuf[:, i * TL + t * TC: i * TL + (t + 1) * TC],
                                         start=(i == 0), stop=(i == C - 1))
                        return r
                    w = [(sDV, v_we), (sDV, ln_rdy), (sGP, ln_rdy_gp)]
                    if dv_h:
                        w.append((sDV, dv_h))
                    sched.add("tensor", hmm, waits=w, incs=[(sPE, 1)])
                    vh = sPE.bump(1)
                    sched.add("vector",
                              (lambda e, t=t, pp=pp:
                               e.tensor_copy(hT_sb[:, t * TC:(t + 1) * TC],
                                             ps_qk[pp][:, 0:TC])),
                              waits=[(sPE, vh)], incs=[(sDV, 1)])
                    dv_h = sDV.bump(1)

                gt_t = {}
                for t in range(NTC):
                    pp = t % 2
                    w = [(sDV, dv_h)]
                    if t >= 2:
                        w.append((sDV, gt_t[t - 2]))
                    sched.add("tensor",
                              (lambda e, t=t, pp=pp:
                               e.matmul(ps_qk[pp][:, 0:TC], seg16_sb[0:8, :],
                                        we_sb[0:8, t * TC:(t + 1) * TC],
                                        start=True, stop=True)),
                              waits=w, incs=[(sPE, 1)])
                    vwb = sPE.bump(1)
                    sched.add("vector",
                              (lambda e, t=t, pp=pp:
                               e.tensor_mul(gTsb[:, t * TC:(t + 1) * TC],
                                            hT_sb[:, t * TC:(t + 1) * TC],
                                            ps_qk[pp][:, 0:TC])),
                              waits=[(sPE, vwb)], incs=[(sDV, 1)])
                    gt_t[t] = sDV.bump(1)
            SYM[f"{tag}_gT_ready"] = sDV.n
            SYM[f"{tag}_s3_pe"] = sPE.n

        def wloop(sched, stack, src_res, out_dram, tag, lnbuf, gTsb):
            gT_rdy = SYM[f"{tag}_gT_ready"]
            ln_rdy_gp = SYM[f"{tag}_ln_ready_gp"]
            s3_pe = SYM[f"{tag}_s3_pe"]
            bm_sb = stack.enter_context(nc.sbuf_tensor(f"{tag}_bm", [128, D], BF16))
            b_sb = stack.enter_context(nc.sbuf_tensor(f"{tag}_b", [1, D], BF16))
            wst = [stack.enter_context(nc.sbuf_tensor(f"{tag}_wst{j}",
                                                      [128, C * 128], BF16))
                   for j in range(2)]
            xres = [stack.enter_context(nc.sbuf_tensor(f"{tag}_xres{j}", [128, TC], F32))
                    for j in range(4)]
            yout = [stack.enter_context(nc.sbuf_tensor(f"{tag}_yout{j}", [128, TC], F32))
                    for j in range(4)]
            ps = [stack.enter_context(nc.psum_tensor(f"{tag}_ps{j}", [128, PSW], F32))
                  for j in range(4)]

            sched.add("gpsimd",
                      (lambda e: e.dma_start(out=bm_sb[:, :], in_=BmT_d.ap())),
                      waits=[(sDV, gT_rdy)], incs=[(sMisc, 16)])
            sMisc.bump(16)
            sched.add("gpsimd",
                      (lambda e: e.dma_start(out=b_sb[:, :], in_=b_f.ap())),
                      incs=[(sMisc, 16)])
            v_bm = sMisc.bump(16)

            pe_done = {}; dv_done = {}; st_done = {}
            w_loaded = {}

            def emit_wload(o):
                wld = [(sDV, gT_rdy)] if o < 2 else [(sPE, pe_done[(o - 2, NTC - 1)])]
                sched.add("gpsimd",
                          (lambda e, o=o:
                           e.dma_start(out=wst[o % 2][:, :].rearrange(
                                           "p (c o) -> p c o", o=128),
                                       in_=WT.ap()[:, o * 128:(o + 1) * 128]
                                       .rearrange("(c p) o -> p c o", p=128))),
                          waits=wld, incs=[(sW[o % 2], 16)])
                return (sW[o % 2], sW[o % 2].bump(16))

            w_loaded[0] = emit_wload(0)
            if C > 1:
                w_loaded[1] = emit_wload(1)

            for o in range(C):
                for t in range(NTC):
                    k = o * NTC + t
                    pp = k % 4
                    pq = k % 4
                    wre = [(sDV, gT_rdy)] if k < 4 else [(sDV, dv_done[k - 4])]
                    sched.add("sync",
                              (lambda e, o=o, t=t, pq=pq, s=src_res:
                               e.dma_start(out=xres[pq][:, :],
                                           in_=s.ap()[o * 128:(o + 1) * 128,
                                                      t * TC:(t + 1) * TC])),
                              waits=wre, incs=[(sXr[pq], 16)])
                    v_x = (sXr[pq], sXr[pq].bump(16))

                    def mmchain(e, o=o, t=t, pp=pp):
                        e.matmul(ps[pp][:, 0:TC], b_sb[0:1, o * 128:(o + 1) * 128],
                                 onesrow[0:1, 0:TC], start=True, stop=False)
                        for i in range(C):
                            e.matmul(ps[pp][:, 0:TC],
                                     wst[o % 2][:, i * 128:(i + 1) * 128],
                                     lnbuf[:, i * TL + t * TC: i * TL + (t + 1) * TC],
                                     start=False, stop=False)
                        return e.matmul(ps[pp][:, 0:TC], bm_sb[:, o * 128:(o + 1) * 128],
                                        gTsb[:, t * TC:(t + 1) * TC],
                                        start=False, stop=True)
                    w = [w_loaded[o], (sMisc, v_bm), (sDV, gT_rdy), (sPE, s3_pe),
                         (sGP, ln_rdy_gp)]
                    if k >= 4:
                        w.append((sDV, dv_done[k - 4]))
                    sched.add("tensor", mmchain, waits=w, incs=[(sPE, 1)])
                    pe_done[(o, t)] = sPE.bump(1)

                    wdv = [(sPE, pe_done[(o, t)]), v_x]
                    if k >= 4:
                        wdv.append(st_done[k - 4])
                    sched.add("vector",
                              (lambda e, pp=pp, pq=pq:
                               e.scalar_tensor_tensor(yout[pq][:, :], ps[pp][:, 0:TC],
                                                      0.5, xres[pq][:, :],
                                                      ALU.mult, ALU.add)),
                              waits=wdv, incs=[(sDV, 1)])
                    dv_done[k] = sDV.bump(1)

                    sched.add("sync",
                              (lambda e, o=o, t=t, pq=pq, od=out_dram:
                               e.dma_start(out=od.ap()[o * 128:(o + 1) * 128,
                                                       t * TC:(t + 1) * TC],
                                           in_=yout[pq][:, :])),
                              waits=[(sDV, dv_done[k])], incs=[(sYo[pq], 16)])
                    st_done[k] = (sYo[pq], sYo[pq].bump(16))
                if o + 2 < C:
                    w_loaded[o + 2] = emit_wload(o + 2)

            def drain(e):
                for s in sYo:
                    if s.n:
                        e.wait_ge(s.h, s.n)
                return None
            sched.add("sync", drain)

        def ffn_app(src, res, out_dram, tag):
            sched = Sched()
            with ExitStack() as ph:
                lnbuf = ph.enter_context(
                    nc.sbuf_tensor(f"{tag}_lnbuf", [128, C * TL], BF16))
                gTsb = ph.enter_context(
                    nc.sbuf_tensor(f"{tag}_gT", [128, TL], BF16))
                with ExitStack() as s12:
                    ln_pass(sched, s12, src, tag, lnbuf)
                with ExitStack() as s3:
                    adapters(sched, s3, tag, lnbuf, gTsb)
                wloop(sched, ph, res, out_dram, tag, lnbuf, gTsb)
                run_block(sched, tag)

        PH = cfg.get("phases", ("g1", "g2", "g3", "g4"))
        if "g1" in PH:
            ffn_app(xT, xT, y1T, "g1")

        def g2_phase():
            sched = Sched()
            with ExitStack() as ph:
                lnbuf = ph.enter_context(
                    nc.sbuf_tensor("g2_lnbuf", [128, C * TL], BF16))
                with ExitStack() as s12:
                    ln_pass(sched, s12, y1T, "g2", lnbuf)
                ln_rdy = SYM["g2_ln_ready"]
                ln_rdy_gp = SYM["g2_ln_ready_gp"]
                norm2 = SYM["g2_norm"]

                bias_sb = ph.enter_context(nc.sbuf_tensor("g2_bias", [1, D], BF16))
                wst = [ph.enter_context(nc.sbuf_tensor(f"g2_wst{j}",
                                                       [128, C * 128], BF16))
                       for j in range(2)]
                pout = [ph.enter_context(nc.sbuf_tensor(f"g2_po{j}",
                                                        [128, max(TC, 256)], BF16))
                        for j in range(4)]
                ps = [ph.enter_context(nc.psum_tensor(f"g2_ps{j}", [128, PSW], F32))
                      for j in range(4)]

                pe_last = {}; dv_done = {}; st_done = {}
                kidx = [0]; oidx = [0]

                v_pre2 = SYM["g2_ln_prenorm"]

                def proj(wsrc, bsrc, out_dram):
                    base = kidx[0]
                    obase = oidx[0]
                    sched.add("gpsimd",
                              (lambda e, b=bsrc:
                               e.dma_start(out=bias_sb[0:1, :], in_=b.ap())),
                              waits=[(sDV, v_pre2), (sPE, sPE.n)],
                              incs=[(sMisc, 16)])
                    v_b = sMisc.bump(16)

                    def emit_wload(o):
                        go = obase + o
                        wld = [(sDV, v_pre2)] if go < 2 else [(sPE, pe_last[go - 2])]
                        sched.add("gpsimd",
                                  (lambda e, o=o, wsrc=wsrc, go=go:
                                   e.dma_start(out=wst[go % 2][:, :].rearrange(
                                                   "p (c o) -> p c o", o=128),
                                               in_=wsrc.ap()[:, o * 128:(o + 1) * 128]
                                               .rearrange("(c p) o -> p c o", p=128))),
                                  waits=wld, incs=[(sW[go % 2], 16)])
                        return (sW[go % 2], sW[go % 2].bump(16))

                    wv = {0: emit_wload(0)}
                    if C > 1:
                        wv[1] = emit_wload(1)
                    for o in range(C):
                        go = obase + o
                        for t in range(NTC):
                            k = base + o * NTC + t
                            pp = k % 4
                            pq = k % 4

                            fg = (go < 4)

                            def mmchain(e, o=o, t=t, pp=pp, ob=go % 2, fg=fg):
                                e.matmul(ps[pp][:, 0:TC],
                                         bias_sb[0:1, o * 128:(o + 1) * 128],
                                         onesrow[0:1, 0:TC], start=True, stop=False)
                                r = None
                                for i in range(C):
                                    if fg:
                                        nsm, nv = norm2[i]
                                        e.wait_ge(nsm.h, nv)
                                    r = e.matmul(
                                        ps[pp][:, 0:TC],
                                        wst[ob][:, i * 128:(i + 1) * 128],
                                        lnbuf[:, i * TL + t * TC: i * TL + (t + 1) * TC],
                                        start=False, stop=(i == C - 1))
                                return r
                            w = [wv[o], (sMisc, v_b)] if fg else \
                                [wv[o], (sMisc, v_b), (sDV, ln_rdy),
                                 (sGP, ln_rdy_gp)]
                            if k >= 4:
                                w.append((sAC, dv_done[k - 4]))
                            sched.add("tensor", mmchain, waits=w, incs=[(sPE, 1)])
                            pe_last[go] = sPE.bump(1)

                            wdv = [(sPE, pe_last[go])]
                            if k >= 4:
                                wdv.append(st_done[k - 4])
                            sched.add("scalar",
                                      (lambda e, pp=pp, pq=pq:
                                       e.activation(pout[pq][:, 0:TC],
                                                    ps[pp][:, 0:TC], AF.Copy)),
                                      waits=wdv, incs=[(sAC, 1)])
                            dv_done[k] = sAC.bump(1)

                            sched.add("sync",
                                      (lambda e, o=o, t=t, pq=pq, od=out_dram:
                                       e.dma_start(
                                           out=od.ap()[o * 128:(o + 1) * 128,
                                                       t * TC:(t + 1) * TC],
                                           in_=pout[pq][:, 0:TC])),
                                      waits=[(sAC, dv_done[k])],
                                      incs=[(sQs[pq], 16)])
                            st_done[k] = (sQs[pq], sQs[pq].bump(16))
                        if o + 2 < C:
                            wv[o + 2] = emit_wload(o + 2)
                    kidx[0] = base + C * NTC
                    oidx[0] = obase + C

                proj(wqT, bq_f, qTs)
                proj(wkT, bk_f, kTs)

                VG = 256
                NVG = D // VG
                NSS = TL // 128
                vst = [ph.enter_context(nc.sbuf_tensor(f"g2_vst{j}",
                                                       [128, C * VG], BF16))
                       for j in range(2)]
                base = kidx[0]
                sched.add("gpsimd",
                          (lambda e: e.dma_start(out=bias_sb[0:1, :], in_=bv_f.ap())),
                          waits=[(sDV, v_pre2), (sPE, sPE.n)], incs=[(sMisc, 16)])
                v_b = sMisc.bump(16)

                def emit_vload(dg):
                    wld = [(sDV, ln_rdy)] if dg < 2 else                         [(sPE, pe_last[("v", dg - 2)])]
                    sched.add("gpsimd",
                              (lambda e, dg=dg:
                               e.dma_start(out=vst[dg % 2][:, :].rearrange(
                                               "p (c v) -> p c v", v=VG),
                                           in_=wvT.ap()[:, dg * VG:(dg + 1) * VG]
                                           .rearrange("(c p) v -> p c v", p=128))),
                              waits=wld, incs=[(sW[dg % 2], 16)])
                    return (sW[dg % 2], sW[dg % 2].bump(16))

                vwv = {0: emit_vload(0)}
                if NVG > 1:
                    vwv[1] = emit_vload(1)
                for dg in range(NVG):
                  for ssg in range(0, NSS, 4):
                    for ss in range(ssg, min(ssg + 4, NSS)):
                        k = base + dg * NSS + ss
                        pp = k % 4
                        pq = k % 4

                        def vmm(e, dg=dg, ss=ss, pp=pp):
                            e.matmul(ps[pp][:, 0:VG], onesrow[0:1, 0:128],
                                     bias_sb[0:1, dg * VG:(dg + 1) * VG],
                                     start=True, stop=False)
                            r = None
                            for i in range(C):
                                r = e.matmul(
                                    ps[pp][:, 0:VG],
                                    lnbuf[:, i * TL + ss * 128: i * TL + (ss + 1) * 128],
                                    vst[dg % 2][:, i * VG:(i + 1) * VG],
                                    start=False, stop=(i == C - 1))
                            return r
                        w = [vwv[dg], (sMisc, v_b), (sDV, ln_rdy),
                             (sGP, ln_rdy_gp)]
                        if k >= 4:
                            w.append((sAC, dv_done[k - 4]))
                        sched.add("tensor", vmm, waits=w, incs=[(sPE, 1)])
                        pe_last[("v", dg)] = sPE.bump(1)

                        wdv = [(sPE, pe_last[("v", dg)])]
                        if k >= 4:
                            wdv.append(st_done[k - 4])
                        sched.add("scalar",
                                  (lambda e, pp=pp, pq=pq:
                                   e.activation(pout[pq][:, 0:VG], ps[pp][:, 0:VG],
                                                AF.Copy)),
                                  waits=wdv, incs=[(sAC, 1)])
                        dv_done[k] = sAC.bump(1)
                    if ssg + 4 >= NSS and dg + 2 < NVG:
                        vwv[dg + 2] = emit_vload(dg + 2)
                    for ss in range(ssg, min(ssg + 4, NSS)):
                        k = base + dg * NSS + ss
                        pq = k % 4
                        sched.add("sync",
                                  (lambda e, dg=dg, ss=ss, pq=pq:
                                   e.dma_start(
                                       out=v_tm.ap()[ss * 128:(ss + 1) * 128,
                                                     dg * VG:(dg + 1) * VG],
                                       in_=pout[pq][:, 0:VG])),
                                  waits=[(sAC, dv_done[k])], incs=[(sQs[pq], 16)])
                        st_done[k] = (sQs[pq], sQs[pq].bump(16))
                kidx[0] = base + NVG * NSS

                def drain_s(e):
                    for s in sQs:
                        if s.n:
                            e.wait_ge(s.h, s.n)
                    return None
                sched.add("sync", drain_s)
                run_block(sched, "g2")

        if "g2" in PH:
            g2_phase()

        def g3_phase():
            sched = Sched()
            with ExitStack() as ph:
                NGR = TB // 64
                qblk = [ph.enter_context(nc.sbuf_tensor(f"g3_qb{j}", [128, 32 * TB],
                                                        BF16)) for j in range(2)]
                kblk = [ph.enter_context(nc.sbuf_tensor(f"g3_kb{j}", [128, 32 * TB],
                                                        BF16)) for j in range(2)]
                vt4 = [ph.enter_context(nc.sbuf_tensor(f"g3_vt{j}", [128, 32 * TB],
                                                       BF16)) for j in range(2)]
                p2bd = [ph.enter_context(nc.sbuf_tensor(f"g3_p2{j}", [128, 2048],
                                                        BF16)) for j in range(4)]
                rsT = ph.enter_context(nc.sbuf_tensor("g3_rsT", [128, 32], F32))
                rzT_sb = ph.enter_context(nc.sbuf_tensor("g3_rzT", [16, 256], F32))
                rz_row = [ph.enter_context(nc.sbuf_tensor(f"g3_rzr{j}", [1, 2048],
                                                          F32)) for j in range(2)]
                ctxe = [ph.enter_context(nc.sbuf_tensor(f"g3_ce{j}", [128, 1024],
                                                        BF16)) for j in range(4)]
                ctxn = ph.enter_context(nc.sbuf_tensor("g3_cn", [128, 32 * TB], BF16))
                y1t = [ph.enter_context(nc.sbuf_tensor(f"g3_y1{j}", [128, TB], F32))
                       for j in range(4)]
                y2t = [ph.enter_context(nc.sbuf_tensor(f"g3_y2{j}", [128, TB], F32))
                       for j in range(4)]
                g_ps = [ph.enter_context(nc.psum_tensor(f"g3_g{j}", [128, PSW], F32))
                        for j in range(2)]
                zT_ps = ph.enter_context(nc.psum_tensor("g3_zT", [128, 32], F32))
                rzrow_ps = ph.enter_context(nc.psum_tensor("g3_rr", [16, 256], F32))
                c_ps = ph.enter_context(nc.psum_tensor("g3_c", [128, 1024], F32))
                rbq_ps = [ph.enter_context(nc.psum_tensor(f"g3_rb{j}", [128, 512],
                                                          F32)) for j in range(2)]

                for j in range(4):
                    sched.add("vector",
                              (lambda e, j=j: e.memset(p2bd[j][:, :], 0.0)),
                              incs=[(sDV, 1)])
                    sDV.bump(1)
                v_ms = sDV.n

                exp_done = {}
                zrec = {}
                zt_done = {}
                tr_done = {}
                ev_done = {}
                d2_done = {}
                rbmm = {}
                rbmm_all = {}
                mult_done = {}
                copy_done = {}
                ctx_done = {}
                gram_done = {}
                pe_blk_done = {}
                y2_st = {}; y2_dv = {}

                def emit_trdma(lin2):
                    gp2 = lin2 % 2
                    w = [(sDV, zrec[lin2])]
                    if lin2 - 2 in ev_done:
                        w.append((sAC, ev_done[lin2 - 2]))
                    sched.add("tensor",
                              (lambda e, gp2=gp2:
                               e.matmul(rzrow_ps[:, gp2 * 128:(gp2 + 1) * 128],
                                        rsT[:, gp2 * 16:(gp2 + 1) * 16],
                                        identfsb[:, :], start=True, stop=True)),
                              waits=w, incs=[(sPE, 1)])
                    tr_done[lin2] = sPE.bump(1)
                    w = [(sPE, tr_done[lin2])]
                    if lin2 - 2 in d2_done:
                        w.append(d2_done[lin2 - 2])
                    sched.add("scalar",
                              (lambda e, gp2=gp2:
                               e.activation(rzT_sb[:, gp2 * 128:(gp2 + 1) * 128],
                                            rzrow_ps[:, gp2 * 128:(gp2 + 1) * 128],
                                            AF.Copy)),
                              waits=w, incs=[(sAC, 1)])
                    ev_done[lin2] = ev_ac = sAC.bump(1)
                    sd2 = sQs[2 + gp2]
                    w = [(sAC, ev_ac)]
                    if lin2 - 2 in rbmm_all:
                        w.append((sPE, rbmm_all[lin2 - 2]))
                    sched.add("sync",
                              (lambda e, gp2=gp2:
                               e.dma_start(
                                   out=rz_row[gp2][0:1, :].rearrange(
                                       "o (j p) -> o j p", p=128),
                                   in_=rzT_sb[:, gp2 * 128:(gp2 + 1) * 128])),
                              waits=w, incs=[(sd2, 16)])
                    d2_done[lin2] = (sd2, sd2.bump(16))

                def emit_rbmult(lin2):
                    g2v = lin2 % NGR
                    gp2 = lin2 % 2
                    for q in range(4):
                        qn = lin2 * 4 + q
                        hn = lin2 * 2 + q // 2
                        w_rb = [d2_done[lin2]]
                        if qn - 2 in mult_done:
                            w_rb.append((sDV, mult_done[qn - 2]))
                        sched.add("tensor",
                                  (lambda e, gp2=gp2, q=q, qn=qn:
                                   e.matmul(rbq_ps[qn % 2][:, :],
                                            ones128f[0:1, :],
                                            rz_row[gp2][0:1, q * 512:(q + 1) * 512],
                                            start=True, stop=True)),
                                  waits=w_rb, incs=[(sPE, 1)])
                        rbmm[qn] = sPE.bump(1)
                        w_ml = [(sPE, rbmm[qn]), (sAC, copy_done[hn])]
                        if g2v == 0 and q == 0 and lin2 >= NGR:
                            w_ml.append((sDV,
                                         y2_dv[(lin2 // NGR - 1) * NH + NH - 1]))
                        sched.add("vector",
                                  (lambda e, g2v=g2v, q=q, qn=qn, hs=hn % 4,
                                   q2=q % 2:
                                   e.tensor_mul(
                                       ctxn[:, g2v * 2048 + q * 512:
                                            g2v * 2048 + (q + 1) * 512],
                                       ctxe[hs][:, q2 * 512:(q2 + 1) * 512],
                                       rbq_ps[qn % 2][:, :])),
                                  waits=w_ml, incs=[(sDV, 1)])
                        mult_done[qn] = sDV.bump(1)
                    rbmm_all[lin2] = rbmm[lin2 * 4 + 3]

                ld_blks = {}

                def emit_block_loads(tb):
                    wld = [] if tb < 2 else \
                        [(sPE, gram_done[(tb - 1) * NGR - 1])]
                    for src_d, dstb in ((qTs, qblk[tb % 2]), (kTs, kblk[tb % 2])):
                        sched.add("sync",
                                  (lambda e, s=src_d, d=dstb, tb=tb:
                                   e.dma_start(
                                       out=d[:, :].rearrange("p (h t) -> p h t", t=TB),
                                       in_=bass.AP(s, tb * TB,
                                                   [[TL, 128], [128 * TL, 32],
                                                    [1, TB]]))),
                                  waits=wld, incs=[(sBlk[tb % 2], 16)])
                        wld = []
                        sBlk[tb % 2].bump(16)
                    w_vt = [] if tb < 2 else \
                        [(sPE, ctx_done.get(((tb - 1) * NGR + NGR - 1, 1), 0))]
                    for tq in range(4):
                        sched.add("sync",
                                  (lambda e, tb=tb, tq=tq:
                                   e.dma_start(
                                       out=vt4[tb % 2][tq * 32:(tq + 1) * 32, :]
                                       .rearrange("h (j d) -> h j d", d=128),
                                       in_=v_tm.ap()[tb * TB:(tb + 1) * TB, :]
                                       .rearrange("(j tq) (h d) -> tq h j d",
                                                  tq=4, h=32)[tq])),
                                  waits=w_vt, incs=[(sBlk[tb % 2], 16)])
                        w_vt = []
                        sBlk[tb % 2].bump(16)
                    ld_blks[tb] = (sBlk[tb % 2], sBlk[tb % 2].n)

                emit_block_loads(0)
                for tb in range(NB):
                    ld_blk = ld_blks[tb]
                    for g in range(NGR):
                        gb = g % 2
                        pb = g % 4
                        lin = tb * NGR + g

                        def gram(e, tb=tb, g=g, gb=gb):
                            r = None
                            ka = kblk[tb % 2][:, :].rearrange(
                                "p (h t) -> p t h", t=TB)
                            qa = qblk[tb % 2][:, :].rearrange(
                                "p (h t) -> p t h", t=TB)
                            for tt in range(64):
                                band = tt % 4
                                tk = g * 64 + tt
                                r = e.matmul(
                                    g_ps[gb][band * 32:(band + 1) * 32,
                                             (tt // 4) * 32:(tt // 4 + 1) * 32],
                                    ka[:, tk, :], qa[:, tk, :],
                                    start=True, stop=True,
                                    tile_position=(0, band * 32))
                            return r
                        w = [ld_blks[tb]]
                        if lin >= 2:
                            w.append((sAC, exp_done[lin - 2]))
                        sched.add("tensor", gram, waits=w, incs=[(sPE, 1)])
                        gram_done[lin] = sPE.bump(1)

                        wexp = [(sPE, gram_done[lin]), (sDV, v_ms)]
                        if lin >= 4:
                            wexp.append((sPE, ctx_done[(lin - 4, 1)]))
                        for band in range(4):
                            in_ap = bass.AP(g_ps[gb], band * 32 * PSW,
                                            [[PSW, 32], [32, 16], [1, 32]])
                            out_ap = bass.AP(p2bd[pb], band * 32 * 2048 + band * 32,
                                             [[2048, 32], [128, 16], [1, 32]])
                            sched.add("scalar",
                                      (lambda e, i=in_ap, o=out_ap:
                                       e.activation(o, i, AF.Exp, scale=sc_mix)),
                                      waits=wexp, incs=[(sAC, 1)])
                            sAC.bump(1)
                        exp_done[lin] = sAC.n

                    for g in range(NGR):
                        pb = g % 4
                        lin = tb * NGR + g
                        gpar = lin % 2

                        w = [(sAC, exp_done[lin])]
                        if lin - 2 in zrec:
                            w.append((sDV, zrec[lin - 2]))

                        def ztmm(e, pb=pb, gpar=gpar):
                            r = None
                            for j in range(16):
                                r = e.matmul(
                                    zT_ps[:, gpar * 16 + j:gpar * 16 + j + 1],
                                    p2bd[pb][:, j * 128:(j + 1) * 128],
                                    onescol[:, 0:1], start=True, stop=True)
                            return r
                        sched.add("tensor", ztmm, waits=w, incs=[(sPE, 1)])
                        zt_done[lin] = sPE.bump(1)

                        w = [(sPE, zt_done[lin])]
                        if lin - 2 in tr_done:
                            w.append((sPE, tr_done[lin - 2]))
                        sched.add("vector",
                                  (lambda e, gpar=gpar:
                                   e.reciprocal(rsT[:, gpar * 16:(gpar + 1) * 16],
                                                zT_ps[:, gpar * 16:
                                                      (gpar + 1) * 16])),
                                  waits=w, incs=[(sDV, 1)])
                        zrec[lin] = sDV.bump(1)

                        if g >= 1:
                            emit_trdma(lin - 1)

                        for half in range(2):
                            hn = lin * 2 + half
                            w = [(sAC, exp_done[lin]), ld_blk]
                            if hn - 1 in copy_done:
                                w.append((sAC, copy_done[hn - 1]))

                            def ctxmm(e, tb=tb, pb=pb, g=g, half=half):
                                r = None
                                vt = vt4[tb % 2]
                                for j in range(8):
                                    jj = half * 8 + j
                                    vcol = (g * 16 + jj) * 128
                                    r = e.matmul(
                                        c_ps[:, j * 128:(j + 1) * 128],
                                        vt[:, vcol:vcol + 128],
                                        p2bd[pb][:, jj * 128:(jj + 1) * 128],
                                        start=True, stop=True)
                                return r
                            sched.add("tensor", ctxmm, waits=w, incs=[(sPE, 1)])
                            ctx_done[(lin, half)] = sPE.bump(1)

                            w = [(sPE, ctx_done[(lin, half)])]
                            if 2 * hn - 7 in mult_done:
                                w.append((sDV, mult_done[2 * hn - 7]))
                            sched.add("scalar",
                                      (lambda e, hs=hn % 4:
                                       e.activation(ctxe[hs][:, :], c_ps[:, 0:1024],
                                                    AF.Copy)),
                                      waits=w, incs=[(sAC, 1)])
                            copy_done[hn] = sAC.bump(1)

                        if g >= 1:
                            emit_rbmult(lin - 1)
                    emit_trdma(tb * NGR + NGR - 1)
                    emit_rbmult(tb * NGR + NGR - 1)
                    pe_blk_done[tb] = sPE.n
                    if tb + 1 < NB:
                        emit_block_loads(tb + 1)

                    v_norm_all = max(mult_done.values())
                    for h in range(NH):
                        kq = tb * NH + h
                        pq = kq % 4
                        wy = [] if kq < 4 else [(sDV, y2_dv[kq - 4])]
                        sched.add("scalar",
                                  (lambda e, h=h, tb=tb, pq=pq:
                                   e.dma_start(out=y1t[pq][:, :],
                                               in_=y1T.ap()[h * 128:(h + 1) * 128,
                                                            tb * TB:(tb + 1) * TB])),
                                  waits=wy, incs=[(sY1[pq], 16)])
                        v_y1 = (sY1[pq], sY1[pq].bump(16))

                        w = [v_y1, (sDV, v_norm_all)]
                        if kq >= 4:
                            w.append(y2_st[kq - 4])
                        ctx_h = ctxn[:, :].rearrange("p (t h) -> p h t",
                                                     h=32)[:, h:h + 1, :].squeeze(1)
                        sched.add("vector",
                                  (lambda e, pq=pq, ctx_h=ctx_h:
                                   e.tensor_add(y2t[pq][:, :], y1t[pq][:, :], ctx_h)),
                                  waits=w, incs=[(sDV, 1)])
                        y2_dv[kq] = sDV.bump(1)

                        sched.add("gpsimd",
                                  (lambda e, h=h, tb=tb, pq=pq:
                                   e.dma_start(out=y2T.ap()[h * 128:(h + 1) * 128,
                                                            tb * TB:(tb + 1) * TB],
                                               in_=y2t[pq][:, :])),
                                  waits=[(sDV, y2_dv[kq])], incs=[(sY2[pq], 16)])
                        y2_st[kq] = (sY2[pq], sY2[pq].bump(16))

                def drain_s(e):
                    for s in sY2:
                        if s.n:
                            e.wait_ge(s.h, s.n)
                    return None
                sched.add("gpsimd", drain_s)
                run_block(sched, "g3")

        if "g3" in PH:
            g3_phase()

        if "g4" in PH:
            ffn_app(y2T, y2T, OUT, "g4")

        sF = Sched()
        sF.add("sync", lambda e: e.nop() and None)
        run_block(sF, "fin")

    return nc


_CFG = dict(D=4096, TL=2048, TC=512, TB=256)
_CACHE = {}


def _seg_consts():
    seg = np.zeros((128, 16), dtype=np.float32)
    for oc in range(2):
        for p in range(128):
            seg[p, oc * 8 + (p // 32 + 4 * oc)] = 1.0
    seg16 = np.zeros((8, 128), dtype=np.float32)
    for e in range(8):
        seg16[e, e * 16:(e + 1) * 16] = 1.0
    return seg.astype(BF), seg16.astype(BF)


def make_inputs(cfg, w, xs):
    D = cfg["D"]; TL = cfg["TL"]
    E, R = cfg.get("E", 8), cfg.get("R", 16)
    seg, seg16 = _seg_consts()
    shared = {
        "WT": np.ascontiguousarray(w["W"].T.astype(BF)),
        "WqT": np.ascontiguousarray(w["Wq"].T.astype(BF)),
        "WkT": np.ascontiguousarray(w["Wk"].T.astype(BF)),
        "AT": np.ascontiguousarray(w["A"].reshape(E * R, D).T.astype(BF)),
        "BmT": np.ascontiguousarray(
            w["Bm"].transpose(0, 2, 1).reshape(E * R, D).astype(BF)),
        "b_f": np.ascontiguousarray(w["b"].reshape(1, D).astype(BF)),
        "wqT": np.ascontiguousarray(w["wq"].T.astype(BF)),
        "wkT": np.ascontiguousarray(w["wk"].T.astype(BF)),
        "wvT": np.ascontiguousarray(w["wv"].T.astype(BF)),
        "bq_f": np.ascontiguousarray(w["bq"].reshape(1, D).astype(BF)),
        "bk_f": np.ascontiguousarray(w["bk"].reshape(1, D).astype(BF)),
        "bv_f": np.ascontiguousarray(w["bv"].reshape(1, D).astype(BF)),
        "onescol": np.ones((128, 1), dtype=BF),
        "onesrow": np.ones((1, 512), dtype=BF),
        "ones128f": np.ones((1, 128), dtype=np.float32),
        "seg": seg,
        "seg16": seg16,
        "ident": np.eye(128, dtype=BF),
        "identf": np.eye(128, dtype=np.float32),
    }
    in_maps = []
    for c in range(8):
        b, hf = c // 2, c % 2
        m = dict(shared)
        m["xT"] = np.ascontiguousarray(xs[b][hf * TL:(hf + 1) * TL, :].T)
        in_maps.append(m)
    return in_maps


def kernel(L0_attn_hidden_states, L0_attn_wq, L0_attn_bq, L0_attn_wk, L0_attn_bk,
           L0_attn_wv, L0_attn_bv, L0_attn_wl, L0_attn_bl, L0_ffn_W, L0_ffn_b,
           L0_ffn_Wq, L0_ffn_Wk, L0_ffn_A, L0_ffn_Bm):
    D = _CFG["D"]; TL = _CFG["TL"]
    x = np.asarray(L0_attn_hidden_states, dtype=np.float32)
    w = {
        "W": np.asarray(L0_ffn_W, dtype=np.float32),
        "b": np.asarray(L0_ffn_b, dtype=np.float32),
        "Wq": np.asarray(L0_ffn_Wq, dtype=np.float32),
        "Wk": np.asarray(L0_ffn_Wk, dtype=np.float32),
        "A": np.asarray(L0_ffn_A, dtype=np.float32),
        "Bm": np.asarray(L0_ffn_Bm, dtype=np.float32),
        "wq": np.asarray(L0_attn_wq, dtype=np.float32),
        "bq": np.asarray(L0_attn_bq, dtype=np.float32),
        "wk": np.asarray(L0_attn_wk, dtype=np.float32),
        "bk": np.asarray(L0_attn_bk, dtype=np.float32),
        "wv": np.asarray(L0_attn_wv, dtype=np.float32),
        "bv": np.asarray(L0_attn_bv, dtype=np.float32),
    }
    xs = [x[bb] for bb in range(4)]
    in_maps = make_inputs(_CFG, w, xs)

    if "nc" not in _CACHE:
        _CACHE["nc"] = build_mk(_CFG)
    res = run_bass_kernel_spmd(_CACHE["nc"], in_maps, core_ids=list(range(8)))

    out = np.empty((4, 2 * TL, D), dtype=np.float32)
    for c in range(8):
        bb, hf = c // 2, c % 2
        out[bb, hf * TL:(hf + 1) * TL, :] = np.asarray(res.results[c]["OUT"]).T
    return out


# version 20
# speedup vs baseline: 1.0321x; 1.0321x over previous
import math
import numpy as np
import ml_dtypes
from contextlib import ExitStack

import concourse.bass as bass
import concourse.mybir as mybir
from concourse.bass_utils import run_bass_kernel_spmd

F32 = mybir.dt.float32
BF16 = mybir.dt.bfloat16
AF = mybir.ActivationFunctionType
ALU = mybir.AluOpType
BF = ml_dtypes.bfloat16


class Sem:
    __slots__ = ("h", "n", "name")

    def __init__(self, h, name):
        self.h = h
        self.n = 0
        self.name = name

    def bump(self, k):
        self.n += k
        return self.n


class Sched:
    def __init__(self):
        self.ops = []

    def add(self, eng, fn, waits=(), incs=()):
        self.ops.append((eng, tuple(w for w in waits if w and w[1] > 0), fn,
                         tuple(incs)))

    def emit(self, eng_name, eng):
        for e, waits, fn, incs in self.ops:
            if e != eng_name:
                continue
            for s, v in waits:
                eng.wait_ge(s.h, v)
            r = fn(eng)
            for s, k in incs:
                assert r is not None
                r = r.then_inc(s.h, k)


def build_mk(cfg):
    D = cfg["D"]; TL = cfg["TL"]; TC = cfg["TC"]
    E = cfg.get("E", 8); DK = cfg.get("DK", 32); R = cfg.get("R", 16)
    C = D // 128
    NH = D // 128
    NTC = TL // TC
    EDK = E * DK
    EOC = EDK // 128
    ER = E * R
    assert EDK == 256 and ER == 128
    assert NH == 32
    TB = cfg.get("TB", 256)
    NB = TL // TB
    NGRP = TB // 16
    eps = 1e-5
    inv_d = 1.0 / D
    sc_mix = 1.0 / math.sqrt(128.0)
    sc_e = 1.0 / math.sqrt(DK)
    PSW = 512

    nc = bass.Bass("TRN2", target_bir_lowering=False,
                   detect_race_conditions=cfg.get("races", True))

    def din(name, shape, dt=F32):
        return nc.dram_tensor(name, shape, dt, kind="ExternalInput")

    xT = din("xT", [D, TL])
    WT = din("WT", [D, D], BF16)
    WqT = din("WqT", [D, EDK], BF16)
    WkT = din("WkT", [D, EDK], BF16)
    AT_d = din("AT", [D, ER], BF16)
    BmT_d = din("BmT", [ER, D], BF16)
    b_f = din("b_f", [1, D], BF16)
    wqT = din("wqT", [D, D], BF16)
    wkT = din("wkT", [D, D], BF16)
    wvT = din("wvT", [D, D], BF16)
    bq_f = din("bq_f", [1, D], BF16)
    bk_f = din("bk_f", [1, D], BF16)
    bv_f = din("bv_f", [1, D], BF16)
    onescol_d = din("onescol", [128, 1], BF16)
    onesrow_d = din("onesrow", [1, 512], BF16)
    ones128f_d = din("ones128f", [1, 128], F32)
    seg_d = din("seg", [128, 16], BF16)
    seg16_d = din("seg16", [8, 128], BF16)
    ident_d = din("ident", [128, 128], BF16)
    identf_d = din("identf", [128, 128])

    y1T = nc.dram_tensor("y1T", [D, TL], F32)
    y2T = nc.dram_tensor("y2T", [D, TL], F32)
    qTs = nc.dram_tensor("qTs", [D, TL], BF16)
    kTs = nc.dram_tensor("kTs", [D, TL], BF16)
    v_tm = nc.dram_tensor("v_tm", [TL, D], BF16)
    OUT = nc.dram_tensor("OUT", [D, TL], F32, kind="ExternalOutput")

    SYM = {}

    with ExitStack() as top:
        def sem(name):
            return Sem(top.enter_context(nc.semaphore(name)), name)

        def sems(name, k):
            return [sem(f"{name}{j}") for j in range(k)]

        sCst = sem("sCst")
        sXi = sems("sXi", 4)
        sXr = sems("sXr", 4)
        sYo = sems("sYo", 4)
        sW = sems("sW", 2)
        sMisc = sem("sMisc")
        sBlk = sems("sBlk", 2)
        sY1 = sems("sY1", 4)
        sY2 = sems("sY2", 4)
        sQs = sems("sQs", 4)
        sPE = sem("sPE"); sDV = sem("sDV"); sAC = sem("sAC")
        sGP = sem("sGP")

        def sbt(name, shape, dt):
            return top.enter_context(nc.sbuf_tensor(name, shape, dt))

        onescol = sbt("onescol_s", [128, 1], BF16)
        onesrow = sbt("onesrow_s", [1, 512], BF16)
        ones128f = sbt("ones128f_s", [1, 128], F32)
        seg_sb = sbt("seg_s", [128, 16], BF16)
        seg16_sb = sbt("seg16_s", [8, 128], BF16)
        identsb = sbt("ident_s", [128, 128], BF16)
        identfsb = sbt("identf_s", [128, 128], F32)
        brsb = sbt("br_s", [128, TL], BF16)
        bnsb = sbt("bn_s", [128, TL], BF16)

        def run_block(sched, name):
            with nc.Block(name) as blk:
                blk.sync(lambda e: sched.emit("sync", e))
                blk.tensor(lambda e: sched.emit("tensor", e))
                blk.vector(lambda e: sched.emit("vector", e))
                blk.scalar(lambda e: sched.emit("scalar", e))
                blk.gpsimd(lambda e: sched.emit("gpsimd", e))

        def addseq(sched, eng, sm, fns, waits=(), inc_final=True):
            base = sm.n
            nsub = len(fns)

            def op(e, fns=tuple(fns), base=base, nsub=nsub):
                last = None
                for j, f in enumerate(fns):
                    if j:
                        e.wait_ge(sm.h, base + j)
                    last = f(e)
                    if j < nsub - 1:
                        last.then_inc(sm.h, 1)
                return last
            sm.bump(nsub - 1)
            incs = [(sm, 1)] if inc_final else []
            sched.add(eng, op, waits, incs)
            if inc_final:
                return sm.bump(1)
            return sm.n

        s0 = Sched()
        for ap_d, ap_s in (
            (onescol_d.ap(), onescol[:, :]),
            (onesrow_d.ap(), onesrow[:, :]),
            (ones128f_d.ap(), ones128f[:, :]),
            (seg_d.ap(), seg_sb[:, :]),
            (seg16_d.ap(), seg16_sb[:, :]),
            (ident_d.ap(), identsb[:, :]),
            (identf_d.ap(), identfsb[:, :]),
        ):
            s0.add("sync",
                   (lambda e, a=ap_s, b=ap_d: e.dma_start(out=a, in_=b)),
                   incs=[(sCst, 16)])
            sCst.bump(16)
        s0.add("sync", lambda e, v=sCst.n: e.wait_ge(sCst.h, v) and None)
        v_cst = sCst.n
        run_block(s0, "consts")

        def ln_pass(sched, stack, src, tag, lnbuf):
            xin = [stack.enter_context(nc.sbuf_tensor(f"{tag}_xin{j}", [128, TL], F32))
                   for j in range(4)]
            xsq = [stack.enter_context(nc.sbuf_tensor(f"{tag}_xsq{j}", [128, TL], BF16))
                   for j in range(2)]
            stats = stack.enter_context(nc.sbuf_tensor(f"{tag}_stats", [1, 3 * TL], F32))
            mean = stats[0:1, 0:TL]
            var = stats[0:1, TL:2 * TL]
            rstd = stats[0:1, 2 * TL:3 * TL]
            negmr = var

            cast_done = [0, 0, 0, 0]; sq_done = [0, 0, 0, 0]; pe_done = [0, 0]
            with nc.psum_tensor(f"{tag}_psm", [1, NTC * PSW], F32) as ps_m, \
                 nc.psum_tensor(f"{tag}_pss", [1, NTC * PSW], F32) as ps_s:
                for i in range(C):
                    pp = i % 4
                    qq = i % 2
                    waits = ([(sDV, cast_done[pp]), (sAC, sq_done[pp])]
                             if i >= 4 else [(sCst, v_cst)])
                    sched.add("sync" if i % 2 == 0 else "gpsimd",
                              (lambda e, a=xin[pp], s=src, i=i:
                               e.dma_start(out=a[:, :],
                                           in_=s.ap()[i * 128:(i + 1) * 128, :])),
                              waits=waits, incs=[(sXi[pp], 16)])
                    v_ld = sXi[pp].bump(16)
                    sched.add("vector",
                              (lambda e, a=xin[pp], i=i:
                               e.tensor_copy(lnbuf[:, i * TL:(i + 1) * TL], a[:, :])),
                              waits=[(sXi[pp], v_ld)], incs=[(sDV, 1)])
                    cast_done[pp] = sDV.bump(1)
                    wsq = [(sXi[pp], v_ld)] + ([(sPE, pe_done[qq])] if i >= 2 else [])
                    sched.add("scalar",
                              (lambda e, a=xsq[qq], b=xin[pp]:
                               e.activation(a[:, :], b[:, :], AF.Square)),
                              waits=wsq, incs=[(sAC, 1)])
                    sq_done[pp] = sAC.bump(1)

                    def statmm(e, i=i, qq=qq):
                        first = (i == 0); last = (i == C - 1)
                        r = None
                        for t in range(NTC):
                            r = e.matmul(ps_m[0:1, t * PSW:t * PSW + TC], onescol[:, :],
                                         lnbuf[:, i * TL + t * TC: i * TL + (t + 1) * TC],
                                         start=first, stop=last)
                        for t in range(NTC):
                            r = e.matmul(ps_s[0:1, t * PSW:t * PSW + TC], onescol[:, :],
                                         xsq[qq][:, t * TC:(t + 1) * TC],
                                         start=first, stop=last)
                        return r
                    sched.add("tensor", statmm,
                              waits=[(sDV, cast_done[pp]), (sAC, sq_done[pp])],
                              incs=[(sPE, 1)])
                    pe_done[qq] = sPE.bump(1)

                v_stats_pe = sPE.n
                psm_ap = bass.AP(ps_m, 0, [[NTC * PSW, 1], [PSW, NTC], [1, TC]])
                pss_ap = bass.AP(ps_s, 0, [[NTC * PSW, 1], [PSW, NTC], [1, TC]])
                mean_ap = mean.rearrange("o (a b) -> o a b", a=NTC)
                var_ap = var.rearrange("o (a b) -> o a b", a=NTC)
                fns = [
                    lambda e: e.tensor_scalar_mul(mean_ap, psm_ap, inv_d),
                    lambda e: e.tensor_scalar(var_ap, pss_ap, inv_d, eps,
                                              ALU.mult, ALU.add),
                    lambda e: e.scalar_tensor_tensor(rstd, mean, -1.0, mean,
                                                     ALU.mult, ALU.mult),
                    lambda e: e.tensor_add(var, var, rstd),
                ]
                v1 = addseq(sched, "vector", sDV, fns, waits=[(sPE, v_stats_pe)])

            sched.add("scalar",
                      (lambda e: e.activation(rstd, var, AF.Sqrt)),
                      waits=[(sDV, v1)], incs=[(sAC, 1)])
            v2 = sAC.bump(1)

            v3 = addseq(sched, "vector", sDV, [
                lambda e: e.reciprocal(rstd, rstd),
                lambda e: e.scalar_tensor_tensor(negmr, mean, -1.0, rstd,
                                                 ALU.mult, ALU.mult),
                lambda e: e.tensor_copy(xsq[0][0:1, 0:TL], rstd),
                lambda e: e.tensor_copy(xsq[1][0:1, 0:TL], negmr),
            ], waits=[(sAC, v2)])

            with nc.psum_tensor(f"{tag}_psr", [128, max(TL, PSW)], F32) as ps_r, \
                 nc.psum_tensor(f"{tag}_psn", [128, max(TL, PSW)], F32) as ps_n:
                def bcasts(e):
                    r = None
                    for t in range(0, TL, 512):
                        w = min(512, TL - t)
                        r = e.matmul(ps_r[:, t:t + w], onesrow[0:1, 0:128],
                                     xsq[0][0:1, t:t + w], start=True, stop=True)
                        r = e.matmul(ps_n[:, t:t + w], onesrow[0:1, 0:128],
                                     xsq[1][0:1, t:t + w],
                                     start=True, stop=True)
                    return r
                sched.add("tensor", bcasts, waits=[(sDV, v3)], incs=[(sPE, 1)])
                v4 = sPE.bump(1)
                sched.add("vector",
                          (lambda e: e.tensor_copy(brsb[:, :], ps_r[:, 0:TL])),
                          waits=[(sPE, v4)], incs=[(sDV, 1)])
                sDV.bump(1)
                sched.add("vector",
                          (lambda e: e.tensor_copy(bnsb[:, :], ps_n[:, 0:TL])),
                          waits=[(sPE, v4)], incs=[(sDV, 1)])
                v_bc = sDV.bump(1)
                SYM[f"{tag}_ln_prenorm"] = v_bc

                norm_done = []
                for i in range(C):
                    sl = lnbuf[:, i * TL:(i + 1) * TL]
                    eng = "vector" if (i % 3) < 2 else "gpsimd"
                    sm = sDV if eng == "vector" else sGP
                    v = addseq(sched, eng, sm, [
                        (lambda e, sl=sl: e.tensor_mul(sl, sl, brsb[:, :])),
                        (lambda e, sl=sl: e.tensor_add(sl, sl, bnsb[:, :])),
                    ], waits=[(sDV, v_bc)])
                    norm_done.append((sm, v))
            SYM[f"{tag}_norm"] = norm_done
            SYM[f"{tag}_ln_ready"] = sDV.n
            SYM[f"{tag}_ln_ready_gp"] = sGP.n

        def adapters(sched, stack, tag, lnbuf, gTsb):
            ln_rdy = SYM[f"{tag}_ln_ready"]
            ln_rdy_gp = SYM[f"{tag}_ln_ready_gp"]
            norm = SYM[f"{tag}_norm"]
            wq_sb = stack.enter_context(nc.sbuf_tensor(f"{tag}_wq", [128, C * EDK], BF16))
            wk_sb = stack.enter_context(nc.sbuf_tensor(f"{tag}_wk", [128, C * EDK], BF16))
            at_sb = stack.enter_context(nc.sbuf_tensor(f"{tag}_at", [128, C * 128], BF16))
            qf_sb = stack.enter_context(nc.sbuf_tensor(f"{tag}_qf", [128, TL], BF16))
            kf_sb = stack.enter_context(nc.sbuf_tensor(f"{tag}_kf", [128, TL], BF16))
            expe = stack.enter_context(nc.sbuf_tensor(f"{tag}_expe", [8, TL], BF16))
            recipe = stack.enter_context(nc.sbuf_tensor(f"{tag}_recipe", [1, TL], F32))
            we_sb = expe
            hT_sb = stack.enter_context(nc.sbuf_tensor(f"{tag}_hT", [128, TL], BF16))

            v_pre = SYM[f"{tag}_ln_prenorm"]
            for dsrc, dst, q in ((WqT, wq_sb, "sync"), (WkT, wk_sb, "sync"),
                                 (AT_d, at_sb, "scalar")):
                w_o = dst.shape[1] // C
                sched.add(q,
                          (lambda e, a=dst, b=dsrc, w_o=w_o:
                           e.dma_start(out=a[:, :].rearrange("p (c o) -> p c o", o=w_o),
                                       in_=b.ap().rearrange("(c p) o -> p c o", p=128))),
                          waits=[(sDV, v_pre)], incs=[(sMisc, 16)])
                sMisc.bump(16)
            v_w = sMisc.n

            with ExitStack() as st3:
                ps_se = [st3.enter_context(nc.psum_tensor(f"{tag}_se{t}", [8, PSW], F32))
                         for t in range(NTC)]
                ps_qk = [st3.enter_context(nc.psum_tensor(f"{tag}_qk{j}", [128, PSW], F32))
                         for j in range(2)]

                ac_evac = 0
                v_prod = 0
                for oc in range(EOC):
                    for wsb, osb in ((wq_sb, qf_sb), (wk_sb, kf_sb)):
                        for t in range(NTC):
                            pp = t % 2

                            fg = (oc == 0)

                            def qkmm(e, oc=oc, wsb=wsb, t=t, pp=pp, fg=fg):
                                r = None
                                for i in range(C):
                                    if fg and i:
                                        nsm, nv = norm[i]
                                        e.wait_ge(nsm.h, nv)
                                    r = e.matmul(
                                        ps_qk[pp][:, 0:TC],
                                        wsb[:, i * EDK + oc * 128: i * EDK + (oc + 1) * 128],
                                        lnbuf[:, i * TL + t * TC: i * TL + (t + 1) * TC],
                                        start=(i == 0), stop=(i == C - 1))
                                return r
                            w = [(sMisc, v_w), norm[0]] if fg else \
                                [(sMisc, v_w), (sDV, ln_rdy), (sGP, ln_rdy_gp)]
                            if ac_evac:
                                w.append((sAC, ac_evac))
                            sched.add("tensor", qkmm, waits=w, incs=[(sPE, 1)])
                            vq = sPE.bump(1)
                            wev = [(sPE, vq)]
                            if v_prod:
                                wev.append((sDV, v_prod))
                            sched.add("scalar",
                                      (lambda e, o=osb, t=t, pp=pp:
                                       e.activation(o[:, t * TC:(t + 1) * TC],
                                                    ps_qk[pp][:, 0:TC], AF.Copy)),
                                      waits=wev, incs=[(sAC, 1)])
                            ac_evac = sAC.bump(1)
                    sched.add("vector",
                              (lambda e: e.tensor_mul(qf_sb[:, :], qf_sb[:, :],
                                                      kf_sb[:, :])),
                              waits=[(sAC, ac_evac)], incs=[(sDV, 1)])
                    v_prod = sDV.bump(1)

                    def segmm(e, oc=oc):
                        r = None
                        for t in range(NTC):
                            r = e.matmul(ps_se[t][0:8, 0:TC],
                                         seg_sb[:, oc * 8:(oc + 1) * 8],
                                         qf_sb[:, t * TC:(t + 1) * TC],
                                         start=(oc == 0), stop=(oc == EOC - 1))
                        return r
                    sched.add("tensor", segmm, waits=[(sDV, v_prod)], incs=[(sPE, 1)])
                    sPE.bump(1)
                v_seg = sPE.n

                def expe_op(e):
                    r = None
                    for t in range(NTC):
                        r = e.activation(expe[0:8, t * TC:(t + 1) * TC],
                                         ps_se[t][0:8, 0:TC], AF.Exp, scale=sc_e)
                    return r
                sched.add("scalar", expe_op, waits=[(sPE, v_seg)], incs=[(sAC, 1)])
                v_expe = sAC.bump(1)

                recip_t = {}
                for t in range(NTC):
                    pp = t % 2
                    w = [(sAC, v_expe), (sAC, ac_evac)]
                    if t >= 2:
                        w.append((sDV, recip_t[t - 2]))
                    sched.add("tensor",
                              (lambda e, t=t, pp=pp:
                               e.matmul(ps_qk[pp][0:1, 0:TC], onescol[0:8, 0:1],
                                        expe[0:8, t * TC:(t + 1) * TC],
                                        start=True, stop=True)),
                              waits=w, incs=[(sPE, 1)])
                    vsm = sPE.bump(1)
                    sched.add("vector",
                              (lambda e, t=t, pp=pp:
                               e.reciprocal(recipe[0:1, t * TC:(t + 1) * TC],
                                            ps_qk[pp][0:1, 0:TC])),
                              waits=[(sPE, vsm)], incs=[(sDV, 1)])
                    recip_t[t] = sDV.bump(1)
                v_recip = sDV.n

                for t in range(NTC):
                    sched.add("tensor",
                              (lambda e, t=t:
                               e.matmul(ps_se[t][0:8, 0:TC], ones128f[0:1, 0:8],
                                        recipe[0:1, t * TC:(t + 1) * TC],
                                        start=True, stop=True)),
                              waits=[(sDV, v_recip), (sAC, v_expe)], incs=[(sPE, 1)])
                    vb8 = sPE.bump(1)
                    sched.add("vector",
                              (lambda e, t=t:
                               e.tensor_mul(we_sb[0:8, t * TC:(t + 1) * TC],
                                            expe[0:8, t * TC:(t + 1) * TC],
                                            ps_se[t][0:8, 0:TC])),
                              waits=[(sPE, vb8)], incs=[(sDV, 1)])
                    sDV.bump(1)
                v_we = sDV.n

                dv_h = 0
                for t in range(NTC):
                    pp = t % 2

                    def hmm(e, t=t, pp=pp):
                        r = None
                        for i in range(C):
                            r = e.matmul(ps_qk[pp][:, 0:TC],
                                         at_sb[:, i * 128:(i + 1) * 128],
                                         lnbuf[:, i * TL + t * TC: i * TL + (t + 1) * TC],
                                         start=(i == 0), stop=(i == C - 1))
                        return r
                    w = [(sDV, v_we), (sDV, ln_rdy), (sGP, ln_rdy_gp)]
                    if dv_h:
                        w.append((sDV, dv_h))
                    sched.add("tensor", hmm, waits=w, incs=[(sPE, 1)])
                    vh = sPE.bump(1)
                    sched.add("vector",
                              (lambda e, t=t, pp=pp:
                               e.tensor_copy(hT_sb[:, t * TC:(t + 1) * TC],
                                             ps_qk[pp][:, 0:TC])),
                              waits=[(sPE, vh)], incs=[(sDV, 1)])
                    dv_h = sDV.bump(1)

                gt_t = {}
                for t in range(NTC):
                    pp = t % 2
                    w = [(sDV, dv_h)]
                    if t >= 2:
                        w.append((sDV, gt_t[t - 2]))
                    sched.add("tensor",
                              (lambda e, t=t, pp=pp:
                               e.matmul(ps_qk[pp][:, 0:TC], seg16_sb[0:8, :],
                                        we_sb[0:8, t * TC:(t + 1) * TC],
                                        start=True, stop=True)),
                              waits=w, incs=[(sPE, 1)])
                    vwb = sPE.bump(1)
                    sched.add("vector",
                              (lambda e, t=t, pp=pp:
                               e.tensor_mul(gTsb[:, t * TC:(t + 1) * TC],
                                            hT_sb[:, t * TC:(t + 1) * TC],
                                            ps_qk[pp][:, 0:TC])),
                              waits=[(sPE, vwb)], incs=[(sDV, 1)])
                    gt_t[t] = sDV.bump(1)
            SYM[f"{tag}_gT_ready"] = sDV.n
            SYM[f"{tag}_s3_pe"] = sPE.n

        def wloop(sched, stack, src_res, out_dram, tag, lnbuf, gTsb):
            gT_rdy = SYM[f"{tag}_gT_ready"]
            ln_rdy_gp = SYM[f"{tag}_ln_ready_gp"]
            s3_pe = SYM[f"{tag}_s3_pe"]
            bm_sb = stack.enter_context(nc.sbuf_tensor(f"{tag}_bm", [128, D], BF16))
            b_sb = stack.enter_context(nc.sbuf_tensor(f"{tag}_b", [1, D], BF16))
            wst = [stack.enter_context(nc.sbuf_tensor(f"{tag}_wst{j}",
                                                      [128, C * 128], BF16))
                   for j in range(2)]
            xres = [stack.enter_context(nc.sbuf_tensor(f"{tag}_xres{j}", [128, TC], F32))
                    for j in range(4)]
            yout = [stack.enter_context(nc.sbuf_tensor(f"{tag}_yout{j}", [128, TC], F32))
                    for j in range(4)]
            ps = [stack.enter_context(nc.psum_tensor(f"{tag}_ps{j}", [128, PSW], F32))
                  for j in range(4)]

            sched.add("gpsimd",
                      (lambda e: e.dma_start(out=bm_sb[:, :], in_=BmT_d.ap())),
                      waits=[(sDV, gT_rdy)], incs=[(sMisc, 16)])
            sMisc.bump(16)
            sched.add("gpsimd",
                      (lambda e: e.dma_start(out=b_sb[:, :], in_=b_f.ap())),
                      incs=[(sMisc, 16)])
            v_bm = sMisc.bump(16)

            pe_done = {}; dv_done = {}; st_done = {}
            w_loaded = {}

            def emit_wload(o):
                wld = [(sDV, gT_rdy)] if o < 2 else [(sPE, pe_done[(o - 2, NTC - 1)])]
                sched.add("gpsimd",
                          (lambda e, o=o:
                           e.dma_start(out=wst[o % 2][:, :].rearrange(
                                           "p (c o) -> p c o", o=128),
                                       in_=WT.ap()[:, o * 128:(o + 1) * 128]
                                       .rearrange("(c p) o -> p c o", p=128))),
                          waits=wld, incs=[(sW[o % 2], 16)])
                return (sW[o % 2], sW[o % 2].bump(16))

            w_loaded[0] = emit_wload(0)
            if C > 1:
                w_loaded[1] = emit_wload(1)

            for o in range(C):
                for t in range(NTC):
                    k = o * NTC + t
                    pp = k % 4
                    pq = k % 4
                    wre = [(sDV, gT_rdy)] if k < 4 else [(sDV, dv_done[k - 4])]
                    sched.add("sync",
                              (lambda e, o=o, t=t, pq=pq, s=src_res:
                               e.dma_start(out=xres[pq][:, :],
                                           in_=s.ap()[o * 128:(o + 1) * 128,
                                                      t * TC:(t + 1) * TC])),
                              waits=wre, incs=[(sXr[pq], 16)])
                    v_x = (sXr[pq], sXr[pq].bump(16))

                    def mmchain(e, o=o, t=t, pp=pp):
                        e.matmul(ps[pp][:, 0:TC], b_sb[0:1, o * 128:(o + 1) * 128],
                                 onesrow[0:1, 0:TC], start=True, stop=False)
                        for i in range(C):
                            e.matmul(ps[pp][:, 0:TC],
                                     wst[o % 2][:, i * 128:(i + 1) * 128],
                                     lnbuf[:, i * TL + t * TC: i * TL + (t + 1) * TC],
                                     start=False, stop=False)
                        return e.matmul(ps[pp][:, 0:TC], bm_sb[:, o * 128:(o + 1) * 128],
                                        gTsb[:, t * TC:(t + 1) * TC],
                                        start=False, stop=True)
                    w = [w_loaded[o], (sMisc, v_bm), (sDV, gT_rdy), (sPE, s3_pe),
                         (sGP, ln_rdy_gp)]
                    if k >= 4:
                        w.append((sDV, dv_done[k - 4]))
                    sched.add("tensor", mmchain, waits=w, incs=[(sPE, 1)])
                    pe_done[(o, t)] = sPE.bump(1)

                    wdv = [(sPE, pe_done[(o, t)]), v_x]
                    if k >= 4:
                        wdv.append(st_done[k - 4])
                    sched.add("vector",
                              (lambda e, pp=pp, pq=pq:
                               e.scalar_tensor_tensor(yout[pq][:, :], ps[pp][:, 0:TC],
                                                      0.5, xres[pq][:, :],
                                                      ALU.mult, ALU.add)),
                              waits=wdv, incs=[(sDV, 1)])
                    dv_done[k] = sDV.bump(1)

                    sched.add("sync",
                              (lambda e, o=o, t=t, pq=pq, od=out_dram:
                               e.dma_start(out=od.ap()[o * 128:(o + 1) * 128,
                                                       t * TC:(t + 1) * TC],
                                           in_=yout[pq][:, :])),
                              waits=[(sDV, dv_done[k])], incs=[(sYo[pq], 16)])
                    st_done[k] = (sYo[pq], sYo[pq].bump(16))
                if o + 2 < C:
                    w_loaded[o + 2] = emit_wload(o + 2)

            def drain(e):
                for s in sYo:
                    if s.n:
                        e.wait_ge(s.h, s.n)
                return None
            sched.add("sync", drain)

        def ffn_app(src, res, out_dram, tag):
            sched = Sched()
            with ExitStack() as ph:
                lnbuf = ph.enter_context(
                    nc.sbuf_tensor(f"{tag}_lnbuf", [128, C * TL], BF16))
                gTsb = ph.enter_context(
                    nc.sbuf_tensor(f"{tag}_gT", [128, TL], BF16))
                with ExitStack() as s12:
                    ln_pass(sched, s12, src, tag, lnbuf)
                with ExitStack() as s3:
                    adapters(sched, s3, tag, lnbuf, gTsb)
                wloop(sched, ph, res, out_dram, tag, lnbuf, gTsb)
                run_block(sched, tag)

        PH = cfg.get("phases", ("g1", "g2", "g3", "g4"))
        if "g1" in PH:
            ffn_app(xT, xT, y1T, "g1")

        def g2_phase():
            sched = Sched()
            with ExitStack() as ph:
                lnbuf = ph.enter_context(
                    nc.sbuf_tensor("g2_lnbuf", [128, C * TL], BF16))
                with ExitStack() as s12:
                    ln_pass(sched, s12, y1T, "g2", lnbuf)
                ln_rdy = SYM["g2_ln_ready"]
                ln_rdy_gp = SYM["g2_ln_ready_gp"]
                norm2 = SYM["g2_norm"]

                bias_sb = ph.enter_context(nc.sbuf_tensor("g2_bias", [1, D], BF16))
                wst = [ph.enter_context(nc.sbuf_tensor(f"g2_wst{j}",
                                                       [128, C * 128], BF16))
                       for j in range(2)]
                pout = [ph.enter_context(nc.sbuf_tensor(f"g2_po{j}",
                                                        [128, max(TC, 256)], BF16))
                        for j in range(4)]
                ps = [ph.enter_context(nc.psum_tensor(f"g2_ps{j}", [128, PSW], F32))
                      for j in range(4)]

                pe_last = {}; dv_done = {}; st_done = {}
                kidx = [0]; oidx = [0]

                v_pre2 = SYM["g2_ln_prenorm"]

                def proj(wsrc, bsrc, out_dram):
                    base = kidx[0]
                    obase = oidx[0]
                    sched.add("sync" if obase == 0 else "gpsimd",
                              (lambda e, b=bsrc:
                               e.dma_start(out=bias_sb[0:1, :], in_=b.ap())),
                              waits=[(sDV, v_pre2), (sPE, sPE.n)],
                              incs=[(sMisc, 16)])
                    v_b = sMisc.bump(16)

                    def emit_wload(o):
                        go = obase + o
                        wld = [(sDV, v_pre2)] if go < 2 else [(sPE, pe_last[go - 2])]
                        sched.add("sync" if go < 2 else "gpsimd",
                                  (lambda e, o=o, wsrc=wsrc, go=go:
                                   e.dma_start(out=wst[go % 2][:, :].rearrange(
                                                   "p (c o) -> p c o", o=128),
                                               in_=wsrc.ap()[:, o * 128:(o + 1) * 128]
                                               .rearrange("(c p) o -> p c o", p=128))),
                                  waits=wld, incs=[(sW[go % 2], 16)])
                        return (sW[go % 2], sW[go % 2].bump(16))

                    wv = {0: emit_wload(0)}
                    if C > 1:
                        wv[1] = emit_wload(1)
                    for o in range(C):
                        go = obase + o
                        for t in range(NTC):
                            k = base + o * NTC + t
                            pp = k % 4
                            pq = k % 4

                            fg = (go < 4)

                            def mmchain(e, o=o, t=t, pp=pp, ob=go % 2, fg=fg):
                                e.matmul(ps[pp][:, 0:TC],
                                         bias_sb[0:1, o * 128:(o + 1) * 128],
                                         onesrow[0:1, 0:TC], start=True, stop=False)
                                r = None
                                for i in range(C):
                                    if fg:
                                        nsm, nv = norm2[i]
                                        e.wait_ge(nsm.h, nv)
                                    r = e.matmul(
                                        ps[pp][:, 0:TC],
                                        wst[ob][:, i * 128:(i + 1) * 128],
                                        lnbuf[:, i * TL + t * TC: i * TL + (t + 1) * TC],
                                        start=False, stop=(i == C - 1))
                                return r
                            w = [wv[o], (sMisc, v_b)] if fg else \
                                [wv[o], (sMisc, v_b), (sDV, ln_rdy),
                                 (sGP, ln_rdy_gp)]
                            if k >= 4:
                                w.append((sAC, dv_done[k - 4]))
                            sched.add("tensor", mmchain, waits=w, incs=[(sPE, 1)])
                            pe_last[go] = sPE.bump(1)

                            wdv = [(sPE, pe_last[go])]
                            if k >= 4:
                                wdv.append(st_done[k - 4])
                            sched.add("scalar",
                                      (lambda e, pp=pp, pq=pq:
                                       e.activation(pout[pq][:, 0:TC],
                                                    ps[pp][:, 0:TC], AF.Copy)),
                                      waits=wdv, incs=[(sAC, 1)])
                            dv_done[k] = sAC.bump(1)

                            sched.add("sync",
                                      (lambda e, o=o, t=t, pq=pq, od=out_dram:
                                       e.dma_start(
                                           out=od.ap()[o * 128:(o + 1) * 128,
                                                       t * TC:(t + 1) * TC],
                                           in_=pout[pq][:, 0:TC])),
                                      waits=[(sAC, dv_done[k])],
                                      incs=[(sQs[pq], 16)])
                            st_done[k] = (sQs[pq], sQs[pq].bump(16))
                        if o + 2 < C:
                            wv[o + 2] = emit_wload(o + 2)
                    kidx[0] = base + C * NTC
                    oidx[0] = obase + C

                proj(wqT, bq_f, qTs)
                proj(wkT, bk_f, kTs)

                VG = 256
                NVG = D // VG
                NSS = TL // 128
                vst = [ph.enter_context(nc.sbuf_tensor(f"g2_vst{j}",
                                                       [128, C * VG], BF16))
                       for j in range(2)]
                base = kidx[0]
                sched.add("gpsimd",
                          (lambda e: e.dma_start(out=bias_sb[0:1, :], in_=bv_f.ap())),
                          waits=[(sDV, v_pre2), (sPE, sPE.n)], incs=[(sMisc, 16)])
                v_b = sMisc.bump(16)

                def emit_vload(dg):
                    wld = [(sDV, ln_rdy)] if dg < 2 else                         [(sPE, pe_last[("v", dg - 2)])]
                    sched.add("gpsimd",
                              (lambda e, dg=dg:
                               e.dma_start(out=vst[dg % 2][:, :].rearrange(
                                               "p (c v) -> p c v", v=VG),
                                           in_=wvT.ap()[:, dg * VG:(dg + 1) * VG]
                                           .rearrange("(c p) v -> p c v", p=128))),
                              waits=wld, incs=[(sW[dg % 2], 16)])
                    return (sW[dg % 2], sW[dg % 2].bump(16))

                vwv = {0: emit_vload(0)}
                if NVG > 1:
                    vwv[1] = emit_vload(1)
                for dg in range(NVG):
                  for ssg in range(0, NSS, 4):
                    for ss in range(ssg, min(ssg + 4, NSS)):
                        k = base + dg * NSS + ss
                        pp = k % 4
                        pq = k % 4

                        def vmm(e, dg=dg, ss=ss, pp=pp):
                            e.matmul(ps[pp][:, 0:VG], onesrow[0:1, 0:128],
                                     bias_sb[0:1, dg * VG:(dg + 1) * VG],
                                     start=True, stop=False)
                            r = None
                            for i in range(C):
                                r = e.matmul(
                                    ps[pp][:, 0:VG],
                                    lnbuf[:, i * TL + ss * 128: i * TL + (ss + 1) * 128],
                                    vst[dg % 2][:, i * VG:(i + 1) * VG],
                                    start=False, stop=(i == C - 1))
                            return r
                        w = [vwv[dg], (sMisc, v_b), (sDV, ln_rdy),
                             (sGP, ln_rdy_gp)]
                        if k >= 4:
                            w.append((sAC, dv_done[k - 4]))
                        sched.add("tensor", vmm, waits=w, incs=[(sPE, 1)])
                        pe_last[("v", dg)] = sPE.bump(1)

                        wdv = [(sPE, pe_last[("v", dg)])]
                        if k >= 4:
                            wdv.append(st_done[k - 4])
                        sched.add("scalar",
                                  (lambda e, pp=pp, pq=pq:
                                   e.activation(pout[pq][:, 0:VG], ps[pp][:, 0:VG],
                                                AF.Copy)),
                                  waits=wdv, incs=[(sAC, 1)])
                        dv_done[k] = sAC.bump(1)
                    if ssg + 4 >= NSS and dg + 2 < NVG:
                        vwv[dg + 2] = emit_vload(dg + 2)
                    for ss in range(ssg, min(ssg + 4, NSS)):
                        k = base + dg * NSS + ss
                        pq = k % 4
                        sched.add("sync",
                                  (lambda e, dg=dg, ss=ss, pq=pq:
                                   e.dma_start(
                                       out=v_tm.ap()[ss * 128:(ss + 1) * 128,
                                                     dg * VG:(dg + 1) * VG],
                                       in_=pout[pq][:, 0:VG])),
                                  waits=[(sAC, dv_done[k])], incs=[(sQs[pq], 16)])
                        st_done[k] = (sQs[pq], sQs[pq].bump(16))
                kidx[0] = base + NVG * NSS

                def drain_s(e):
                    for s in sQs:
                        if s.n:
                            e.wait_ge(s.h, s.n)
                    return None
                sched.add("sync", drain_s)
                run_block(sched, "g2")

        if "g2" in PH:
            g2_phase()

        def g3_phase():
            sched = Sched()
            with ExitStack() as ph:
                NGR = TB // 64
                qblk = [ph.enter_context(nc.sbuf_tensor(f"g3_qb{j}", [128, 32 * TB],
                                                        BF16)) for j in range(2)]
                kblk = [ph.enter_context(nc.sbuf_tensor(f"g3_kb{j}", [128, 32 * TB],
                                                        BF16)) for j in range(2)]
                vt4 = [ph.enter_context(nc.sbuf_tensor(f"g3_vt{j}", [128, 32 * TB],
                                                       BF16)) for j in range(2)]
                p2bd = [ph.enter_context(nc.sbuf_tensor(f"g3_p2{j}", [128, 2048],
                                                        BF16)) for j in range(4)]
                rsT = ph.enter_context(nc.sbuf_tensor("g3_rsT", [128, 32], F32))
                rzT_sb = ph.enter_context(nc.sbuf_tensor("g3_rzT", [16, 256], F32))
                rz_row = [ph.enter_context(nc.sbuf_tensor(f"g3_rzr{j}", [1, 2048],
                                                          F32)) for j in range(2)]
                ctxe = [ph.enter_context(nc.sbuf_tensor(f"g3_ce{j}", [128, 1024],
                                                        BF16)) for j in range(4)]
                ctxn = ph.enter_context(nc.sbuf_tensor("g3_cn", [128, 32 * TB], BF16))
                y1t = [ph.enter_context(nc.sbuf_tensor(f"g3_y1{j}", [128, TB], F32))
                       for j in range(4)]
                y2t = [ph.enter_context(nc.sbuf_tensor(f"g3_y2{j}", [128, TB], F32))
                       for j in range(4)]
                g_ps = [ph.enter_context(nc.psum_tensor(f"g3_g{j}", [128, PSW], F32))
                        for j in range(2)]
                zT_ps = ph.enter_context(nc.psum_tensor("g3_zT", [128, 32], F32))
                rzrow_ps = ph.enter_context(nc.psum_tensor("g3_rr", [16, 256], F32))
                c_ps = ph.enter_context(nc.psum_tensor("g3_c", [128, 1024], F32))
                rbq_ps = [ph.enter_context(nc.psum_tensor(f"g3_rb{j}", [128, 512],
                                                          F32)) for j in range(2)]

                for j in range(4):
                    sched.add("vector",
                              (lambda e, j=j: e.memset(p2bd[j][:, :], 0.0)),
                              incs=[(sDV, 1)])
                    sDV.bump(1)
                v_ms = sDV.n

                exp_done = {}
                zrec = {}
                zt_done = {}
                tr_done = {}
                ev_done = {}
                d2_done = {}
                rbmm = {}
                rbmm_all = {}
                mult_done = {}
                copy_done = {}
                ctx_done = {}
                gram_done = {}
                pe_blk_done = {}
                y2_st = {}; y2_dv = {}

                def emit_trdma(lin2):
                    gp2 = lin2 % 2
                    w = [(sDV, zrec[lin2])]
                    if lin2 - 2 in ev_done:
                        w.append((sAC, ev_done[lin2 - 2]))
                    sched.add("tensor",
                              (lambda e, gp2=gp2:
                               e.matmul(rzrow_ps[:, gp2 * 128:(gp2 + 1) * 128],
                                        rsT[:, gp2 * 16:(gp2 + 1) * 16],
                                        identfsb[:, :], start=True, stop=True)),
                              waits=w, incs=[(sPE, 1)])
                    tr_done[lin2] = sPE.bump(1)
                    w = [(sPE, tr_done[lin2])]
                    if lin2 - 2 in d2_done:
                        w.append(d2_done[lin2 - 2])
                    sched.add("scalar",
                              (lambda e, gp2=gp2:
                               e.activation(rzT_sb[:, gp2 * 128:(gp2 + 1) * 128],
                                            rzrow_ps[:, gp2 * 128:(gp2 + 1) * 128],
                                            AF.Copy)),
                              waits=w, incs=[(sAC, 1)])
                    ev_done[lin2] = ev_ac = sAC.bump(1)
                    sd2 = sQs[2 + gp2]
                    w = [(sAC, ev_ac)]
                    if lin2 - 2 in rbmm_all:
                        w.append((sPE, rbmm_all[lin2 - 2]))
                    sched.add("sync",
                              (lambda e, gp2=gp2:
                               e.dma_start(
                                   out=rz_row[gp2][0:1, :].rearrange(
                                       "o (j p) -> o j p", p=128),
                                   in_=rzT_sb[:, gp2 * 128:(gp2 + 1) * 128])),
                              waits=w, incs=[(sd2, 16)])
                    d2_done[lin2] = (sd2, sd2.bump(16))

                def emit_rbmult(lin2):
                    g2v = lin2 % NGR
                    gp2 = lin2 % 2
                    for q in range(4):
                        qn = lin2 * 4 + q
                        hn = lin2 * 2 + q // 2
                        w_rb = [d2_done[lin2]]
                        if qn - 2 in mult_done:
                            w_rb.append((sDV, mult_done[qn - 2]))
                        sched.add("tensor",
                                  (lambda e, gp2=gp2, q=q, qn=qn:
                                   e.matmul(rbq_ps[qn % 2][:, :],
                                            ones128f[0:1, :],
                                            rz_row[gp2][0:1, q * 512:(q + 1) * 512],
                                            start=True, stop=True)),
                                  waits=w_rb, incs=[(sPE, 1)])
                        rbmm[qn] = sPE.bump(1)
                        w_ml = [(sPE, rbmm[qn]), (sAC, copy_done[hn])]
                        if g2v == 0 and q == 0 and lin2 >= NGR:
                            w_ml.append((sDV,
                                         y2_dv[(lin2 // NGR - 1) * NH + NH - 1]))
                        sched.add("vector",
                                  (lambda e, g2v=g2v, q=q, qn=qn, hs=hn % 4,
                                   q2=q % 2:
                                   e.tensor_mul(
                                       ctxn[:, g2v * 2048 + q * 512:
                                            g2v * 2048 + (q + 1) * 512],
                                       ctxe[hs][:, q2 * 512:(q2 + 1) * 512],
                                       rbq_ps[qn % 2][:, :])),
                                  waits=w_ml, incs=[(sDV, 1)])
                        mult_done[qn] = sDV.bump(1)
                    rbmm_all[lin2] = rbmm[lin2 * 4 + 3]

                ld_blks = {}

                def emit_block_loads(tb):
                    wld = [] if tb < 2 else \
                        [(sPE, gram_done[(tb - 1) * NGR - 1])]
                    for src_d, dstb in ((qTs, qblk[tb % 2]), (kTs, kblk[tb % 2])):
                        sched.add("sync",
                                  (lambda e, s=src_d, d=dstb, tb=tb:
                                   e.dma_start(
                                       out=d[:, :].rearrange("p (h t) -> p h t", t=TB),
                                       in_=bass.AP(s, tb * TB,
                                                   [[TL, 128], [128 * TL, 32],
                                                    [1, TB]]))),
                                  waits=wld, incs=[(sBlk[tb % 2], 16)])
                        wld = []
                        sBlk[tb % 2].bump(16)
                    w_vt = [] if tb < 2 else \
                        [(sPE, ctx_done.get(((tb - 1) * NGR + NGR - 1, 1), 0))]
                    for tq in range(4):
                        sched.add("sync",
                                  (lambda e, tb=tb, tq=tq:
                                   e.dma_start(
                                       out=vt4[tb % 2][tq * 32:(tq + 1) * 32, :]
                                       .rearrange("h (j d) -> h j d", d=128),
                                       in_=v_tm.ap()[tb * TB:(tb + 1) * TB, :]
                                       .rearrange("(j tq) (h d) -> tq h j d",
                                                  tq=4, h=32)[tq])),
                                  waits=w_vt, incs=[(sBlk[tb % 2], 16)])
                        w_vt = []
                        sBlk[tb % 2].bump(16)
                    ld_blks[tb] = (sBlk[tb % 2], sBlk[tb % 2].n)

                emit_block_loads(0)
                for tb in range(NB):
                    ld_blk = ld_blks[tb]
                    for g in range(NGR):
                        gb = g % 2
                        pb = g % 4
                        lin = tb * NGR + g

                        def gram(e, tb=tb, g=g, gb=gb):
                            r = None
                            ka = kblk[tb % 2][:, :].rearrange(
                                "p (h t) -> p t h", t=TB)
                            qa = qblk[tb % 2][:, :].rearrange(
                                "p (h t) -> p t h", t=TB)
                            for tt in range(64):
                                band = tt % 4
                                tk = g * 64 + tt
                                r = e.matmul(
                                    g_ps[gb][band * 32:(band + 1) * 32,
                                             (tt // 4) * 32:(tt // 4 + 1) * 32],
                                    ka[:, tk, :], qa[:, tk, :],
                                    start=True, stop=True,
                                    tile_position=(0, band * 32))
                            return r
                        w = [ld_blks[tb]]
                        if lin >= 2:
                            w.append((sAC, exp_done[lin - 2]))
                        sched.add("tensor", gram, waits=w, incs=[(sPE, 1)])
                        gram_done[lin] = sPE.bump(1)

                        wexp = [(sPE, gram_done[lin]), (sDV, v_ms)]
                        if lin >= 4:
                            wexp.append((sPE, ctx_done[(lin - 4, 1)]))
                        for band in range(4):
                            in_ap = bass.AP(g_ps[gb], band * 32 * PSW,
                                            [[PSW, 32], [32, 16], [1, 32]])
                            out_ap = bass.AP(p2bd[pb], band * 32 * 2048 + band * 32,
                                             [[2048, 32], [128, 16], [1, 32]])
                            sched.add("scalar",
                                      (lambda e, i=in_ap, o=out_ap:
                                       e.activation(o, i, AF.Exp, scale=sc_mix)),
                                      waits=wexp, incs=[(sAC, 1)])
                            sAC.bump(1)
                        exp_done[lin] = sAC.n

                    for g in range(NGR):
                        pb = g % 4
                        lin = tb * NGR + g
                        gpar = lin % 2

                        w = [(sAC, exp_done[lin])]
                        if lin - 2 in zrec:
                            w.append((sDV, zrec[lin - 2]))

                        def ztmm(e, pb=pb, gpar=gpar):
                            r = None
                            for j in range(16):
                                r = e.matmul(
                                    zT_ps[:, gpar * 16 + j:gpar * 16 + j + 1],
                                    p2bd[pb][:, j * 128:(j + 1) * 128],
                                    onescol[:, 0:1], start=True, stop=True)
                            return r
                        sched.add("tensor", ztmm, waits=w, incs=[(sPE, 1)])
                        zt_done[lin] = sPE.bump(1)

                        w = [(sPE, zt_done[lin])]
                        if lin - 2 in tr_done:
                            w.append((sPE, tr_done[lin - 2]))
                        sched.add("vector",
                                  (lambda e, gpar=gpar:
                                   e.reciprocal(rsT[:, gpar * 16:(gpar + 1) * 16],
                                                zT_ps[:, gpar * 16:
                                                      (gpar + 1) * 16])),
                                  waits=w, incs=[(sDV, 1)])
                        zrec[lin] = sDV.bump(1)

                        if g >= 1:
                            emit_trdma(lin - 1)

                        for half in range(2):
                            hn = lin * 2 + half
                            w = [(sAC, exp_done[lin]), ld_blk]
                            if hn - 1 in copy_done:
                                w.append((sAC, copy_done[hn - 1]))

                            def ctxmm(e, tb=tb, pb=pb, g=g, half=half):
                                r = None
                                vt = vt4[tb % 2]
                                for j in range(8):
                                    jj = half * 8 + j
                                    vcol = (g * 16 + jj) * 128
                                    r = e.matmul(
                                        c_ps[:, j * 128:(j + 1) * 128],
                                        vt[:, vcol:vcol + 128],
                                        p2bd[pb][:, jj * 128:(jj + 1) * 128],
                                        start=True, stop=True)
                                return r
                            sched.add("tensor", ctxmm, waits=w, incs=[(sPE, 1)])
                            ctx_done[(lin, half)] = sPE.bump(1)

                            w = [(sPE, ctx_done[(lin, half)])]
                            if 2 * hn - 7 in mult_done:
                                w.append((sDV, mult_done[2 * hn - 7]))
                            sched.add("scalar",
                                      (lambda e, hs=hn % 4:
                                       e.activation(ctxe[hs][:, :], c_ps[:, 0:1024],
                                                    AF.Copy)),
                                      waits=w, incs=[(sAC, 1)])
                            copy_done[hn] = sAC.bump(1)

                        if g >= 1:
                            emit_rbmult(lin - 1)
                    emit_trdma(tb * NGR + NGR - 1)
                    emit_rbmult(tb * NGR + NGR - 1)
                    pe_blk_done[tb] = sPE.n
                    if tb + 1 < NB:
                        emit_block_loads(tb + 1)

                    v_norm_all = max(mult_done.values())
                    for h in range(NH):
                        kq = tb * NH + h
                        pq = kq % 4
                        wy = [] if kq < 4 else [(sDV, y2_dv[kq - 4])]
                        sched.add("scalar",
                                  (lambda e, h=h, tb=tb, pq=pq:
                                   e.dma_start(out=y1t[pq][:, :],
                                               in_=y1T.ap()[h * 128:(h + 1) * 128,
                                                            tb * TB:(tb + 1) * TB])),
                                  waits=wy, incs=[(sY1[pq], 16)])
                        v_y1 = (sY1[pq], sY1[pq].bump(16))

                        w = [v_y1, (sDV, v_norm_all)]
                        if kq >= 4:
                            w.append(y2_st[kq - 4])
                        ctx_h = ctxn[:, :].rearrange("p (t h) -> p h t",
                                                     h=32)[:, h:h + 1, :].squeeze(1)
                        sched.add("vector",
                                  (lambda e, pq=pq, ctx_h=ctx_h:
                                   e.tensor_add(y2t[pq][:, :], y1t[pq][:, :], ctx_h)),
                                  waits=w, incs=[(sDV, 1)])
                        y2_dv[kq] = sDV.bump(1)

                        sched.add("gpsimd",
                                  (lambda e, h=h, tb=tb, pq=pq:
                                   e.dma_start(out=y2T.ap()[h * 128:(h + 1) * 128,
                                                            tb * TB:(tb + 1) * TB],
                                               in_=y2t[pq][:, :])),
                                  waits=[(sDV, y2_dv[kq])], incs=[(sY2[pq], 16)])
                        y2_st[kq] = (sY2[pq], sY2[pq].bump(16))

                def drain_s(e):
                    for s in sY2:
                        if s.n:
                            e.wait_ge(s.h, s.n)
                    return None
                sched.add("gpsimd", drain_s)
                run_block(sched, "g3")

        if "g3" in PH:
            g3_phase()

        if "g4" in PH:
            ffn_app(y2T, y2T, OUT, "g4")

        sF = Sched()
        sF.add("sync", lambda e: e.nop() and None)
        run_block(sF, "fin")

    return nc


_CFG = dict(D=4096, TL=2048, TC=512, TB=256)
_CACHE = {}


def _seg_consts():
    seg = np.zeros((128, 16), dtype=np.float32)
    for oc in range(2):
        for p in range(128):
            seg[p, oc * 8 + (p // 32 + 4 * oc)] = 1.0
    seg16 = np.zeros((8, 128), dtype=np.float32)
    for e in range(8):
        seg16[e, e * 16:(e + 1) * 16] = 1.0
    return seg.astype(BF), seg16.astype(BF)


def make_inputs(cfg, w, xs):
    D = cfg["D"]; TL = cfg["TL"]
    E, R = cfg.get("E", 8), cfg.get("R", 16)
    seg, seg16 = _seg_consts()
    shared = {
        "WT": np.ascontiguousarray(w["W"].T.astype(BF)),
        "WqT": np.ascontiguousarray(w["Wq"].T.astype(BF)),
        "WkT": np.ascontiguousarray(w["Wk"].T.astype(BF)),
        "AT": np.ascontiguousarray(w["A"].reshape(E * R, D).T.astype(BF)),
        "BmT": np.ascontiguousarray(
            w["Bm"].transpose(0, 2, 1).reshape(E * R, D).astype(BF)),
        "b_f": np.ascontiguousarray(w["b"].reshape(1, D).astype(BF)),
        "wqT": np.ascontiguousarray(w["wq"].T.astype(BF)),
        "wkT": np.ascontiguousarray(w["wk"].T.astype(BF)),
        "wvT": np.ascontiguousarray(w["wv"].T.astype(BF)),
        "bq_f": np.ascontiguousarray(w["bq"].reshape(1, D).astype(BF)),
        "bk_f": np.ascontiguousarray(w["bk"].reshape(1, D).astype(BF)),
        "bv_f": np.ascontiguousarray(w["bv"].reshape(1, D).astype(BF)),
        "onescol": np.ones((128, 1), dtype=BF),
        "onesrow": np.ones((1, 512), dtype=BF),
        "ones128f": np.ones((1, 128), dtype=np.float32),
        "seg": seg,
        "seg16": seg16,
        "ident": np.eye(128, dtype=BF),
        "identf": np.eye(128, dtype=np.float32),
    }
    in_maps = []
    for c in range(8):
        b, hf = c // 2, c % 2
        m = dict(shared)
        m["xT"] = np.ascontiguousarray(xs[b][hf * TL:(hf + 1) * TL, :].T)
        in_maps.append(m)
    return in_maps


def kernel(L0_attn_hidden_states, L0_attn_wq, L0_attn_bq, L0_attn_wk, L0_attn_bk,
           L0_attn_wv, L0_attn_bv, L0_attn_wl, L0_attn_bl, L0_ffn_W, L0_ffn_b,
           L0_ffn_Wq, L0_ffn_Wk, L0_ffn_A, L0_ffn_Bm):
    D = _CFG["D"]; TL = _CFG["TL"]
    x = np.asarray(L0_attn_hidden_states, dtype=np.float32)
    w = {
        "W": np.asarray(L0_ffn_W, dtype=np.float32),
        "b": np.asarray(L0_ffn_b, dtype=np.float32),
        "Wq": np.asarray(L0_ffn_Wq, dtype=np.float32),
        "Wk": np.asarray(L0_ffn_Wk, dtype=np.float32),
        "A": np.asarray(L0_ffn_A, dtype=np.float32),
        "Bm": np.asarray(L0_ffn_Bm, dtype=np.float32),
        "wq": np.asarray(L0_attn_wq, dtype=np.float32),
        "bq": np.asarray(L0_attn_bq, dtype=np.float32),
        "wk": np.asarray(L0_attn_wk, dtype=np.float32),
        "bk": np.asarray(L0_attn_bk, dtype=np.float32),
        "wv": np.asarray(L0_attn_wv, dtype=np.float32),
        "bv": np.asarray(L0_attn_bv, dtype=np.float32),
    }
    xs = [x[bb] for bb in range(4)]
    in_maps = make_inputs(_CFG, w, xs)

    if "nc" not in _CACHE:
        _CACHE["nc"] = build_mk(_CFG)
    res = run_bass_kernel_spmd(_CACHE["nc"], in_maps, core_ids=list(range(8)))

    out = np.empty((4, 2 * TL, D), dtype=np.float32)
    for c in range(8):
        bb, hf = c // 2, c % 2
        out[bb, hf * TL:(hf + 1) * TL, :] = np.asarray(res.results[c]["OUT"]).T
    return out


# version 21 (final)
# speedup vs baseline: 1.0574x; 1.0246x over previous
import math
import numpy as np
import ml_dtypes
from contextlib import ExitStack

import concourse.bass as bass
import concourse.mybir as mybir
from concourse.bass_utils import run_bass_kernel_spmd

F32 = mybir.dt.float32
BF16 = mybir.dt.bfloat16
AF = mybir.ActivationFunctionType
ALU = mybir.AluOpType
BF = ml_dtypes.bfloat16


class Sem:
    __slots__ = ("h", "n", "name")

    def __init__(self, h, name):
        self.h = h
        self.n = 0
        self.name = name

    def bump(self, k):
        self.n += k
        return self.n


class Sched:
    def __init__(self):
        self.ops = []

    def add(self, eng, fn, waits=(), incs=()):
        self.ops.append((eng, tuple(w for w in waits if w and w[1] > 0), fn,
                         tuple(incs)))

    def emit(self, eng_name, eng):
        for e, waits, fn, incs in self.ops:
            if e != eng_name:
                continue
            for s, v in waits:
                eng.wait_ge(s.h, v)
            r = fn(eng)
            for s, k in incs:
                assert r is not None
                r = r.then_inc(s.h, k)


def build_mk(cfg):
    D = cfg["D"]; TL = cfg["TL"]; TC = cfg["TC"]
    E = cfg.get("E", 8); DK = cfg.get("DK", 32); R = cfg.get("R", 16)
    C = D // 128
    NH = D // 128
    NTC = TL // TC
    EDK = E * DK
    EOC = EDK // 128
    ER = E * R
    assert EDK == 256 and ER == 128
    assert NH == 32
    TB = cfg.get("TB", 256)
    NB = TL // TB
    NGRP = TB // 16
    eps = 1e-5
    inv_d = 1.0 / D
    sc_mix = 1.0 / math.sqrt(128.0)
    sc_e = 1.0 / math.sqrt(DK)
    PSW = 512

    nc = bass.Bass("TRN2", target_bir_lowering=False,
                   detect_race_conditions=cfg.get("races", True))

    def din(name, shape, dt=F32):
        return nc.dram_tensor(name, shape, dt, kind="ExternalInput")

    xT = din("xT", [D, TL])
    WT = din("WT", [D, D], BF16)
    WqT = din("WqT", [D, EDK], BF16)
    WkT = din("WkT", [D, EDK], BF16)
    AT_d = din("AT", [D, ER], BF16)
    BmT_d = din("BmT", [ER, D], BF16)
    b_f = din("b_f", [1, D], BF16)
    wqT = din("wqT", [D, D], BF16)
    wkT = din("wkT", [D, D], BF16)
    wvT = din("wvT", [D, D], BF16)
    bq_f = din("bq_f", [1, D], BF16)
    bk_f = din("bk_f", [1, D], BF16)
    bv_f = din("bv_f", [1, D], BF16)
    onescol_d = din("onescol", [128, 1], BF16)
    onesrow_d = din("onesrow", [1, 512], BF16)
    ones128f_d = din("ones128f", [1, 128], F32)
    seg_d = din("seg", [128, 16], BF16)
    seg16_d = din("seg16", [8, 128], BF16)
    ident_d = din("ident", [128, 128], BF16)
    identf_d = din("identf", [128, 128])

    y1T = nc.dram_tensor("y1T", [D, TL], F32)
    y2T = nc.dram_tensor("y2T", [D, TL], F32)
    qTs = nc.dram_tensor("qTs", [D, TL], BF16)
    kTs = nc.dram_tensor("kTs", [D, TL], BF16)
    v_tm = nc.dram_tensor("v_tm", [TL, D], BF16)
    OUT = nc.dram_tensor("OUT", [D, TL], F32, kind="ExternalOutput")

    SYM = {}

    with ExitStack() as top:
        def sem(name):
            return Sem(top.enter_context(nc.semaphore(name)), name)

        def sems(name, k):
            return [sem(f"{name}{j}") for j in range(k)]

        sCst = sem("sCst")
        sXi = sems("sXi", 4)
        sXr = sems("sXr", 4)
        sYo = sems("sYo", 4)
        sW = sems("sW", 2)
        sMisc = sem("sMisc")
        sBlk = sems("sBlk", 2)
        sY1 = sems("sY1", 4)
        sY2 = sems("sY2", 4)
        sQs = sems("sQs", 4)
        sPE = sem("sPE"); sDV = sem("sDV"); sAC = sem("sAC")
        sGP = sem("sGP")

        def sbt(name, shape, dt):
            return top.enter_context(nc.sbuf_tensor(name, shape, dt))

        onescol = sbt("onescol_s", [128, 1], BF16)
        onesrow = sbt("onesrow_s", [1, 512], BF16)
        ones128f = sbt("ones128f_s", [1, 128], F32)
        seg_sb = sbt("seg_s", [128, 16], BF16)
        seg16_sb = sbt("seg16_s", [8, 128], BF16)
        identsb = sbt("ident_s", [128, 128], BF16)
        identfsb = sbt("identf_s", [128, 128], F32)
        brsb = sbt("br_s", [128, TL], BF16)
        bnsb = sbt("bn_s", [128, TL], BF16)

        def run_block(sched, name):
            with nc.Block(name) as blk:
                blk.sync(lambda e: sched.emit("sync", e))
                blk.tensor(lambda e: sched.emit("tensor", e))
                blk.vector(lambda e: sched.emit("vector", e))
                blk.scalar(lambda e: sched.emit("scalar", e))
                blk.gpsimd(lambda e: sched.emit("gpsimd", e))

        def addseq(sched, eng, sm, fns, waits=(), inc_final=True):
            base = sm.n
            nsub = len(fns)

            def op(e, fns=tuple(fns), base=base, nsub=nsub):
                last = None
                for j, f in enumerate(fns):
                    if j:
                        e.wait_ge(sm.h, base + j)
                    last = f(e)
                    if j < nsub - 1:
                        last.then_inc(sm.h, 1)
                return last
            sm.bump(nsub - 1)
            incs = [(sm, 1)] if inc_final else []
            sched.add(eng, op, waits, incs)
            if inc_final:
                return sm.bump(1)
            return sm.n

        s0 = Sched()
        for ap_d, ap_s in (
            (onescol_d.ap(), onescol[:, :]),
            (onesrow_d.ap(), onesrow[:, :]),
            (ones128f_d.ap(), ones128f[:, :]),
            (seg_d.ap(), seg_sb[:, :]),
            (seg16_d.ap(), seg16_sb[:, :]),
            (ident_d.ap(), identsb[:, :]),
            (identf_d.ap(), identfsb[:, :]),
        ):
            s0.add("sync",
                   (lambda e, a=ap_s, b=ap_d: e.dma_start(out=a, in_=b)),
                   incs=[(sCst, 16)])
            sCst.bump(16)
        s0.add("sync", lambda e, v=sCst.n: e.wait_ge(sCst.h, v) and None)
        v_cst = sCst.n
        run_block(s0, "consts")

        def ln_pass(sched, stack, src, tag, lnbuf):
            xin = [stack.enter_context(nc.sbuf_tensor(f"{tag}_xin{j}", [128, TL], F32))
                   for j in range(4)]
            xsq = [stack.enter_context(nc.sbuf_tensor(f"{tag}_xsq{j}", [128, TL], BF16))
                   for j in range(2)]
            stats = stack.enter_context(nc.sbuf_tensor(f"{tag}_stats", [1, 3 * TL], F32))
            mean = stats[0:1, 0:TL]
            var = stats[0:1, TL:2 * TL]
            rstd = stats[0:1, 2 * TL:3 * TL]
            negmr = var

            cast_done = [0, 0, 0, 0]; sq_done = [0, 0, 0, 0]; pe_done = [0, 0]
            with nc.psum_tensor(f"{tag}_psm", [1, NTC * PSW], F32) as ps_m, \
                 nc.psum_tensor(f"{tag}_pss", [1, NTC * PSW], F32) as ps_s:
                for i in range(C):
                    pp = i % 4
                    qq = i % 2
                    waits = ([(sDV, cast_done[pp]), (sAC, sq_done[pp])]
                             if i >= 4 else [(sCst, v_cst)])
                    sched.add("sync" if i % 2 == 0 else "gpsimd",
                              (lambda e, a=xin[pp], s=src, i=i:
                               e.dma_start(out=a[:, :],
                                           in_=s.ap()[i * 128:(i + 1) * 128, :])),
                              waits=waits, incs=[(sXi[pp], 16)])
                    v_ld = sXi[pp].bump(16)
                    sched.add("vector",
                              (lambda e, a=xin[pp], i=i:
                               e.tensor_copy(lnbuf[:, i * TL:(i + 1) * TL], a[:, :])),
                              waits=[(sXi[pp], v_ld)], incs=[(sDV, 1)])
                    cast_done[pp] = sDV.bump(1)
                    wsq = [(sXi[pp], v_ld)] + ([(sPE, pe_done[qq])] if i >= 2 else [])
                    sched.add("scalar",
                              (lambda e, a=xsq[qq], b=xin[pp]:
                               e.activation(a[:, :], b[:, :], AF.Square)),
                              waits=wsq, incs=[(sAC, 1)])
                    sq_done[pp] = sAC.bump(1)

                    def statmm(e, i=i, qq=qq):
                        first = (i == 0); last = (i == C - 1)
                        r = None
                        for t in range(NTC):
                            r = e.matmul(ps_m[0:1, t * PSW:t * PSW + TC], onescol[:, :],
                                         lnbuf[:, i * TL + t * TC: i * TL + (t + 1) * TC],
                                         start=first, stop=last)
                        for t in range(NTC):
                            r = e.matmul(ps_s[0:1, t * PSW:t * PSW + TC], onescol[:, :],
                                         xsq[qq][:, t * TC:(t + 1) * TC],
                                         start=first, stop=last)
                        return r
                    sched.add("tensor", statmm,
                              waits=[(sDV, cast_done[pp]), (sAC, sq_done[pp])],
                              incs=[(sPE, 1)])
                    pe_done[qq] = sPE.bump(1)

                v_stats_pe = sPE.n
                psm_ap = bass.AP(ps_m, 0, [[NTC * PSW, 1], [PSW, NTC], [1, TC]])
                pss_ap = bass.AP(ps_s, 0, [[NTC * PSW, 1], [PSW, NTC], [1, TC]])
                mean_ap = mean.rearrange("o (a b) -> o a b", a=NTC)
                var_ap = var.rearrange("o (a b) -> o a b", a=NTC)
                fns = [
                    lambda e: e.tensor_scalar_mul(mean_ap, psm_ap, inv_d),
                    lambda e: e.tensor_scalar(var_ap, pss_ap, inv_d, eps,
                                              ALU.mult, ALU.add),
                    lambda e: e.scalar_tensor_tensor(rstd, mean, -1.0, mean,
                                                     ALU.mult, ALU.mult),
                    lambda e: e.tensor_add(var, var, rstd),
                ]
                v1 = addseq(sched, "vector", sDV, fns, waits=[(sPE, v_stats_pe)])

            sched.add("scalar",
                      (lambda e: e.activation(rstd, var, AF.Sqrt)),
                      waits=[(sDV, v1)], incs=[(sAC, 1)])
            v2 = sAC.bump(1)

            v3 = addseq(sched, "vector", sDV, [
                lambda e: e.reciprocal(rstd, rstd),
                lambda e: e.scalar_tensor_tensor(negmr, mean, -1.0, rstd,
                                                 ALU.mult, ALU.mult),
                lambda e: e.tensor_copy(xsq[0][0:1, 0:TL], rstd),
                lambda e: e.tensor_copy(xsq[1][0:1, 0:TL], negmr),
            ], waits=[(sAC, v2)])

            with nc.psum_tensor(f"{tag}_psr", [128, max(TL, PSW)], F32) as ps_r, \
                 nc.psum_tensor(f"{tag}_psn", [128, max(TL, PSW)], F32) as ps_n:
                def bcasts(e):
                    r = None
                    for t in range(0, TL, 512):
                        w = min(512, TL - t)
                        r = e.matmul(ps_r[:, t:t + w], onesrow[0:1, 0:128],
                                     xsq[0][0:1, t:t + w], start=True, stop=True)
                        r = e.matmul(ps_n[:, t:t + w], onesrow[0:1, 0:128],
                                     xsq[1][0:1, t:t + w],
                                     start=True, stop=True)
                    return r
                sched.add("tensor", bcasts, waits=[(sDV, v3)], incs=[(sPE, 1)])
                v4 = sPE.bump(1)
                sched.add("vector",
                          (lambda e: e.tensor_copy(brsb[:, :], ps_r[:, 0:TL])),
                          waits=[(sPE, v4)], incs=[(sDV, 1)])
                sDV.bump(1)
                sched.add("vector",
                          (lambda e: e.tensor_copy(bnsb[:, :], ps_n[:, 0:TL])),
                          waits=[(sPE, v4)], incs=[(sDV, 1)])
                v_bc = sDV.bump(1)
                SYM[f"{tag}_ln_prenorm"] = v_bc

                norm_done = []
                for i in range(C):
                    sl = lnbuf[:, i * TL:(i + 1) * TL]
                    eng = "vector" if (i % 3) < 2 else "gpsimd"
                    sm = sDV if eng == "vector" else sGP
                    v = addseq(sched, eng, sm, [
                        (lambda e, sl=sl: e.tensor_mul(sl, sl, brsb[:, :])),
                        (lambda e, sl=sl: e.tensor_add(sl, sl, bnsb[:, :])),
                    ], waits=[(sDV, v_bc)])
                    norm_done.append((sm, v))
            SYM[f"{tag}_norm"] = norm_done
            SYM[f"{tag}_ln_ready"] = sDV.n
            SYM[f"{tag}_ln_ready_gp"] = sGP.n

        def adapters(sched, stack, tag, lnbuf, gTsb):
            ln_rdy = SYM[f"{tag}_ln_ready"]
            ln_rdy_gp = SYM[f"{tag}_ln_ready_gp"]
            norm = SYM[f"{tag}_norm"]
            wq_sb = stack.enter_context(nc.sbuf_tensor(f"{tag}_wq", [128, C * EDK], BF16))
            wk_sb = stack.enter_context(nc.sbuf_tensor(f"{tag}_wk", [128, C * EDK], BF16))
            at_sb = stack.enter_context(nc.sbuf_tensor(f"{tag}_at", [128, C * 128], BF16))
            qf_sb = stack.enter_context(nc.sbuf_tensor(f"{tag}_qf", [128, TL], BF16))
            kf_sb = stack.enter_context(nc.sbuf_tensor(f"{tag}_kf", [128, TL], BF16))
            expe = stack.enter_context(nc.sbuf_tensor(f"{tag}_expe", [8, TL], BF16))
            recipe = stack.enter_context(nc.sbuf_tensor(f"{tag}_recipe", [1, TL], F32))
            we_sb = expe
            hT_sb = stack.enter_context(nc.sbuf_tensor(f"{tag}_hT", [128, TL], BF16))

            v_pre = SYM[f"{tag}_ln_prenorm"]
            for dsrc, dst, q in ((WqT, wq_sb, "sync"), (WkT, wk_sb, "sync"),
                                 (AT_d, at_sb, "scalar")):
                w_o = dst.shape[1] // C
                sched.add(q,
                          (lambda e, a=dst, b=dsrc, w_o=w_o:
                           e.dma_start(out=a[:, :].rearrange("p (c o) -> p c o", o=w_o),
                                       in_=b.ap().rearrange("(c p) o -> p c o", p=128))),
                          waits=[(sDV, v_pre)], incs=[(sMisc, 16)])
                sMisc.bump(16)
            v_w = sMisc.n

            with ExitStack() as st3:
                ps_se = [st3.enter_context(nc.psum_tensor(f"{tag}_se{t}", [8, PSW], F32))
                         for t in range(NTC)]
                ps_qk = [st3.enter_context(nc.psum_tensor(f"{tag}_qk{j}", [128, PSW], F32))
                         for j in range(2)]

                ac_evac = 0
                v_prod = 0
                for oc in range(EOC):
                    for wsb, osb in ((wq_sb, qf_sb), (wk_sb, kf_sb)):
                        for t in range(NTC):
                            pp = t % 2

                            fg = (oc == 0)

                            def qkmm(e, oc=oc, wsb=wsb, t=t, pp=pp, fg=fg):
                                r = None
                                for i in range(C):
                                    if fg and i:
                                        nsm, nv = norm[i]
                                        e.wait_ge(nsm.h, nv)
                                    r = e.matmul(
                                        ps_qk[pp][:, 0:TC],
                                        wsb[:, i * EDK + oc * 128: i * EDK + (oc + 1) * 128],
                                        lnbuf[:, i * TL + t * TC: i * TL + (t + 1) * TC],
                                        start=(i == 0), stop=(i == C - 1))
                                return r
                            w = [(sMisc, v_w), norm[0]] if fg else \
                                [(sMisc, v_w), (sDV, ln_rdy), (sGP, ln_rdy_gp)]
                            if ac_evac:
                                w.append((sAC, ac_evac))
                            sched.add("tensor", qkmm, waits=w, incs=[(sPE, 1)])
                            vq = sPE.bump(1)
                            wev = [(sPE, vq)]
                            if v_prod:
                                wev.append((sDV, v_prod))
                            sched.add("scalar",
                                      (lambda e, o=osb, t=t, pp=pp:
                                       e.activation(o[:, t * TC:(t + 1) * TC],
                                                    ps_qk[pp][:, 0:TC], AF.Copy)),
                                      waits=wev, incs=[(sAC, 1)])
                            ac_evac = sAC.bump(1)
                    sched.add("vector",
                              (lambda e: e.tensor_mul(qf_sb[:, :], qf_sb[:, :],
                                                      kf_sb[:, :])),
                              waits=[(sAC, ac_evac)], incs=[(sDV, 1)])
                    v_prod = sDV.bump(1)

                    def segmm(e, oc=oc):
                        r = None
                        for t in range(NTC):
                            r = e.matmul(ps_se[t][0:8, 0:TC],
                                         seg_sb[:, oc * 8:(oc + 1) * 8],
                                         qf_sb[:, t * TC:(t + 1) * TC],
                                         start=(oc == 0), stop=(oc == EOC - 1))
                        return r
                    sched.add("tensor", segmm, waits=[(sDV, v_prod)], incs=[(sPE, 1)])
                    sPE.bump(1)
                v_seg = sPE.n

                def expe_op(e):
                    r = None
                    for t in range(NTC):
                        r = e.activation(expe[0:8, t * TC:(t + 1) * TC],
                                         ps_se[t][0:8, 0:TC], AF.Exp, scale=sc_e)
                    return r
                sched.add("scalar", expe_op, waits=[(sPE, v_seg)], incs=[(sAC, 1)])
                v_expe = sAC.bump(1)

                recip_t = {}
                for t in range(NTC):
                    pp = t % 2
                    w = [(sAC, v_expe), (sAC, ac_evac)]
                    if t >= 2:
                        w.append((sDV, recip_t[t - 2]))
                    sched.add("tensor",
                              (lambda e, t=t, pp=pp:
                               e.matmul(ps_qk[pp][0:1, 0:TC], onescol[0:8, 0:1],
                                        expe[0:8, t * TC:(t + 1) * TC],
                                        start=True, stop=True)),
                              waits=w, incs=[(sPE, 1)])
                    vsm = sPE.bump(1)
                    sched.add("vector",
                              (lambda e, t=t, pp=pp:
                               e.reciprocal(recipe[0:1, t * TC:(t + 1) * TC],
                                            ps_qk[pp][0:1, 0:TC])),
                              waits=[(sPE, vsm)], incs=[(sDV, 1)])
                    recip_t[t] = sDV.bump(1)
                v_recip = sDV.n

                for t in range(NTC):
                    sched.add("tensor",
                              (lambda e, t=t:
                               e.matmul(ps_se[t][0:8, 0:TC], ones128f[0:1, 0:8],
                                        recipe[0:1, t * TC:(t + 1) * TC],
                                        start=True, stop=True)),
                              waits=[(sDV, v_recip), (sAC, v_expe)], incs=[(sPE, 1)])
                    vb8 = sPE.bump(1)
                    sched.add("vector",
                              (lambda e, t=t:
                               e.tensor_mul(we_sb[0:8, t * TC:(t + 1) * TC],
                                            expe[0:8, t * TC:(t + 1) * TC],
                                            ps_se[t][0:8, 0:TC])),
                              waits=[(sPE, vb8)], incs=[(sDV, 1)])
                    sDV.bump(1)
                v_we = sDV.n

                dv_h = 0
                for t in range(NTC):
                    pp = t % 2

                    def hmm(e, t=t, pp=pp):
                        r = None
                        for i in range(C):
                            r = e.matmul(ps_qk[pp][:, 0:TC],
                                         at_sb[:, i * 128:(i + 1) * 128],
                                         lnbuf[:, i * TL + t * TC: i * TL + (t + 1) * TC],
                                         start=(i == 0), stop=(i == C - 1))
                        return r
                    w = [(sDV, v_we), (sDV, ln_rdy), (sGP, ln_rdy_gp)]
                    if dv_h:
                        w.append((sDV, dv_h))
                    sched.add("tensor", hmm, waits=w, incs=[(sPE, 1)])
                    vh = sPE.bump(1)
                    sched.add("vector",
                              (lambda e, t=t, pp=pp:
                               e.tensor_copy(hT_sb[:, t * TC:(t + 1) * TC],
                                             ps_qk[pp][:, 0:TC])),
                              waits=[(sPE, vh)], incs=[(sDV, 1)])
                    dv_h = sDV.bump(1)

                gt_t = {}
                for t in range(NTC):
                    pp = t % 2
                    w = [(sDV, dv_h)]
                    if t >= 2:
                        w.append((sDV, gt_t[t - 2]))
                    sched.add("tensor",
                              (lambda e, t=t, pp=pp:
                               e.matmul(ps_qk[pp][:, 0:TC], seg16_sb[0:8, :],
                                        we_sb[0:8, t * TC:(t + 1) * TC],
                                        start=True, stop=True)),
                              waits=w, incs=[(sPE, 1)])
                    vwb = sPE.bump(1)
                    sched.add("vector",
                              (lambda e, t=t, pp=pp:
                               e.tensor_mul(gTsb[:, t * TC:(t + 1) * TC],
                                            hT_sb[:, t * TC:(t + 1) * TC],
                                            ps_qk[pp][:, 0:TC])),
                              waits=[(sPE, vwb)], incs=[(sDV, 1)])
                    gt_t[t] = sDV.bump(1)
            SYM[f"{tag}_gT_ready"] = sDV.n
            SYM[f"{tag}_s3_pe"] = sPE.n

        def wloop(sched, stack, src_res, out_dram, tag, lnbuf, gTsb):
            gT_rdy = SYM[f"{tag}_gT_ready"]
            ln_rdy_gp = SYM[f"{tag}_ln_ready_gp"]
            s3_pe = SYM[f"{tag}_s3_pe"]
            bm_sb = stack.enter_context(nc.sbuf_tensor(f"{tag}_bm", [128, D], BF16))
            b_sb = stack.enter_context(nc.sbuf_tensor(f"{tag}_b", [1, D], BF16))
            wst = [stack.enter_context(nc.sbuf_tensor(f"{tag}_wst{j}",
                                                      [128, C * 128], BF16))
                   for j in range(2)]
            xres = [stack.enter_context(nc.sbuf_tensor(f"{tag}_xres{j}", [128, TC], F32))
                    for j in range(4)]
            yout = [stack.enter_context(nc.sbuf_tensor(f"{tag}_yout{j}", [128, TC], F32))
                    for j in range(4)]
            ps = [stack.enter_context(nc.psum_tensor(f"{tag}_ps{j}", [128, PSW], F32))
                  for j in range(4)]

            sched.add("gpsimd",
                      (lambda e: e.dma_start(out=bm_sb[:, :], in_=BmT_d.ap())),
                      waits=[(sDV, gT_rdy)], incs=[(sMisc, 16)])
            sMisc.bump(16)
            sched.add("gpsimd",
                      (lambda e: e.dma_start(out=b_sb[:, :], in_=b_f.ap())),
                      incs=[(sMisc, 16)])
            v_bm = sMisc.bump(16)

            pe_done = {}; dv_done = {}; st_done = {}
            w_loaded = {}

            def emit_wload(o):
                wld = [(sDV, gT_rdy)] if o < 2 else [(sPE, pe_done[(o - 2, NTC - 1)])]
                sched.add("gpsimd",
                          (lambda e, o=o:
                           e.dma_start(out=wst[o % 2][:, :].rearrange(
                                           "p (c o) -> p c o", o=128),
                                       in_=WT.ap()[:, o * 128:(o + 1) * 128]
                                       .rearrange("(c p) o -> p c o", p=128))),
                          waits=wld, incs=[(sW[o % 2], 16)])
                return (sW[o % 2], sW[o % 2].bump(16))

            w_loaded[0] = emit_wload(0)
            if C > 1:
                w_loaded[1] = emit_wload(1)

            for o in range(C):
                for t in range(NTC):
                    k = o * NTC + t
                    pp = k % 4
                    pq = k % 4
                    wre = [(sDV, gT_rdy)] if k < 4 else [(sDV, dv_done[k - 4])]
                    sched.add("sync",
                              (lambda e, o=o, t=t, pq=pq, s=src_res:
                               e.dma_start(out=xres[pq][:, :],
                                           in_=s.ap()[o * 128:(o + 1) * 128,
                                                      t * TC:(t + 1) * TC])),
                              waits=wre, incs=[(sXr[pq], 16)])
                    v_x = (sXr[pq], sXr[pq].bump(16))

                    def mmchain(e, o=o, t=t, pp=pp):
                        e.matmul(ps[pp][:, 0:TC], b_sb[0:1, o * 128:(o + 1) * 128],
                                 onesrow[0:1, 0:TC], start=True, stop=False)
                        for i in range(C):
                            e.matmul(ps[pp][:, 0:TC],
                                     wst[o % 2][:, i * 128:(i + 1) * 128],
                                     lnbuf[:, i * TL + t * TC: i * TL + (t + 1) * TC],
                                     start=False, stop=False)
                        return e.matmul(ps[pp][:, 0:TC], bm_sb[:, o * 128:(o + 1) * 128],
                                        gTsb[:, t * TC:(t + 1) * TC],
                                        start=False, stop=True)
                    w = [w_loaded[o], (sMisc, v_bm), (sDV, gT_rdy), (sPE, s3_pe),
                         (sGP, ln_rdy_gp)]
                    if k >= 4:
                        w.append((sDV, dv_done[k - 4]))
                    sched.add("tensor", mmchain, waits=w, incs=[(sPE, 1)])
                    pe_done[(o, t)] = sPE.bump(1)

                    wdv = [(sPE, pe_done[(o, t)]), v_x]
                    if k >= 4:
                        wdv.append(st_done[k - 4])
                    sched.add("vector",
                              (lambda e, pp=pp, pq=pq:
                               e.scalar_tensor_tensor(yout[pq][:, :], ps[pp][:, 0:TC],
                                                      0.5, xres[pq][:, :],
                                                      ALU.mult, ALU.add)),
                              waits=wdv, incs=[(sDV, 1)])
                    dv_done[k] = sDV.bump(1)

                    sched.add("sync",
                              (lambda e, o=o, t=t, pq=pq, od=out_dram:
                               e.dma_start(out=od.ap()[o * 128:(o + 1) * 128,
                                                       t * TC:(t + 1) * TC],
                                           in_=yout[pq][:, :])),
                              waits=[(sDV, dv_done[k])], incs=[(sYo[pq], 16)])
                    st_done[k] = (sYo[pq], sYo[pq].bump(16))
                if o + 2 < C:
                    w_loaded[o + 2] = emit_wload(o + 2)

            def drain(e):
                for s in sYo:
                    if s.n:
                        e.wait_ge(s.h, s.n)
                return None
            sched.add("sync", drain)

        def ffn_app(src, res, out_dram, tag):
            sched = Sched()
            with ExitStack() as ph:
                lnbuf = ph.enter_context(
                    nc.sbuf_tensor(f"{tag}_lnbuf", [128, C * TL], BF16))
                gTsb = ph.enter_context(
                    nc.sbuf_tensor(f"{tag}_gT", [128, TL], BF16))
                with ExitStack() as s12:
                    ln_pass(sched, s12, src, tag, lnbuf)
                with ExitStack() as s3:
                    adapters(sched, s3, tag, lnbuf, gTsb)
                wloop(sched, ph, res, out_dram, tag, lnbuf, gTsb)
                run_block(sched, tag)

        PH = cfg.get("phases", ("g1", "g2", "g3", "g4"))
        if "g1" in PH:
            ffn_app(xT, xT, y1T, "g1")

        def g2_phase():
            sched = Sched()
            with ExitStack() as ph:
                lnbuf = ph.enter_context(
                    nc.sbuf_tensor("g2_lnbuf", [128, C * TL], BF16))
                with ExitStack() as s12:
                    ln_pass(sched, s12, y1T, "g2", lnbuf)
                ln_rdy = SYM["g2_ln_ready"]
                ln_rdy_gp = SYM["g2_ln_ready_gp"]
                norm2 = SYM["g2_norm"]

                bias_sb = ph.enter_context(nc.sbuf_tensor("g2_bias", [1, D], BF16))
                wst = [ph.enter_context(nc.sbuf_tensor(f"g2_wst{j}",
                                                       [128, C * 128], BF16))
                       for j in range(2)]
                pout = [ph.enter_context(nc.sbuf_tensor(f"g2_po{j}",
                                                        [128, max(TC, 256)], BF16))
                        for j in range(4)]
                ps = [ph.enter_context(nc.psum_tensor(f"g2_ps{j}", [128, PSW], F32))
                      for j in range(4)]

                pe_last = {}; dv_done = {}; st_done = {}
                kidx = [0]; oidx = [0]

                v_pre2 = SYM["g2_ln_prenorm"]

                def proj(wsrc, bsrc, out_dram):
                    base = kidx[0]
                    obase = oidx[0]
                    sched.add("sync" if obase == 0 else "gpsimd",
                              (lambda e, b=bsrc:
                               e.dma_start(out=bias_sb[0:1, :], in_=b.ap())),
                              waits=[(sDV, v_pre2), (sPE, sPE.n)],
                              incs=[(sMisc, 16)])
                    v_b = sMisc.bump(16)

                    def emit_wload(o):
                        go = obase + o
                        wld = [(sDV, v_pre2)] if go < 2 else [(sPE, pe_last[go - 2])]
                        sched.add("sync" if go < 2 else "gpsimd",
                                  (lambda e, o=o, wsrc=wsrc, go=go:
                                   e.dma_start(out=wst[go % 2][:, :].rearrange(
                                                   "p (c o) -> p c o", o=128),
                                               in_=wsrc.ap()[:, o * 128:(o + 1) * 128]
                                               .rearrange("(c p) o -> p c o", p=128))),
                                  waits=wld, incs=[(sW[go % 2], 16)])
                        return (sW[go % 2], sW[go % 2].bump(16))

                    wv = {0: emit_wload(0)}
                    if C > 1:
                        wv[1] = emit_wload(1)
                    for o in range(C):
                        go = obase + o
                        for t in range(NTC):
                            k = base + o * NTC + t
                            pp = k % 4
                            pq = k % 4

                            fg = (go < 4)

                            def mmchain(e, o=o, t=t, pp=pp, ob=go % 2, fg=fg):
                                e.matmul(ps[pp][:, 0:TC],
                                         bias_sb[0:1, o * 128:(o + 1) * 128],
                                         onesrow[0:1, 0:TC], start=True, stop=False)
                                r = None
                                for i in range(C):
                                    if fg:
                                        nsm, nv = norm2[i]
                                        e.wait_ge(nsm.h, nv)
                                    r = e.matmul(
                                        ps[pp][:, 0:TC],
                                        wst[ob][:, i * 128:(i + 1) * 128],
                                        lnbuf[:, i * TL + t * TC: i * TL + (t + 1) * TC],
                                        start=False, stop=(i == C - 1))
                                return r
                            w = [wv[o], (sMisc, v_b)] if fg else \
                                [wv[o], (sMisc, v_b), (sDV, ln_rdy),
                                 (sGP, ln_rdy_gp)]
                            if k >= 4:
                                w.append((sAC, dv_done[k - 4]))
                            sched.add("tensor", mmchain, waits=w, incs=[(sPE, 1)])
                            pe_last[go] = sPE.bump(1)

                            wdv = [(sPE, pe_last[go])]
                            if k >= 4:
                                wdv.append(st_done[k - 4])
                            sched.add("scalar",
                                      (lambda e, pp=pp, pq=pq:
                                       e.activation(pout[pq][:, 0:TC],
                                                    ps[pp][:, 0:TC], AF.Copy)),
                                      waits=wdv, incs=[(sAC, 1)])
                            dv_done[k] = sAC.bump(1)

                            sched.add("sync",
                                      (lambda e, o=o, t=t, pq=pq, od=out_dram:
                                       e.dma_start(
                                           out=od.ap()[o * 128:(o + 1) * 128,
                                                       t * TC:(t + 1) * TC],
                                           in_=pout[pq][:, 0:TC])),
                                      waits=[(sAC, dv_done[k])],
                                      incs=[(sQs[pq], 16)])
                            st_done[k] = (sQs[pq], sQs[pq].bump(16))
                        if o + 2 < C:
                            wv[o + 2] = emit_wload(o + 2)
                    kidx[0] = base + C * NTC
                    oidx[0] = obase + C

                proj(wqT, bq_f, qTs)
                proj(wkT, bk_f, kTs)

                VG = 256
                NVG = D // VG
                NSS = TL // 128
                vst = [ph.enter_context(nc.sbuf_tensor(f"g2_vst{j}",
                                                       [128, C * VG], BF16))
                       for j in range(2)]
                base = kidx[0]
                sched.add("gpsimd",
                          (lambda e: e.dma_start(out=bias_sb[0:1, :], in_=bv_f.ap())),
                          waits=[(sDV, v_pre2), (sPE, sPE.n)], incs=[(sMisc, 16)])
                v_b = sMisc.bump(16)

                def emit_vload(dg):
                    wld = [(sDV, ln_rdy)] if dg < 2 else                         [(sPE, pe_last[("v", dg - 2)])]
                    sched.add("gpsimd",
                              (lambda e, dg=dg:
                               e.dma_start(out=vst[dg % 2][:, :].rearrange(
                                               "p (c v) -> p c v", v=VG),
                                           in_=wvT.ap()[:, dg * VG:(dg + 1) * VG]
                                           .rearrange("(c p) v -> p c v", p=128))),
                              waits=wld, incs=[(sW[dg % 2], 16)])
                    return (sW[dg % 2], sW[dg % 2].bump(16))

                vwv = {0: emit_vload(0)}
                if NVG > 1:
                    vwv[1] = emit_vload(1)
                for dg in range(NVG):
                  for ssg in range(0, NSS, 4):
                    for ss in range(ssg, min(ssg + 4, NSS)):
                        k = base + dg * NSS + ss
                        pp = k % 4
                        pq = k % 4

                        def vmm(e, dg=dg, ss=ss, pp=pp):
                            e.matmul(ps[pp][:, 0:VG], onesrow[0:1, 0:128],
                                     bias_sb[0:1, dg * VG:(dg + 1) * VG],
                                     start=True, stop=False)
                            r = None
                            for i in range(C):
                                r = e.matmul(
                                    ps[pp][:, 0:VG],
                                    lnbuf[:, i * TL + ss * 128: i * TL + (ss + 1) * 128],
                                    vst[dg % 2][:, i * VG:(i + 1) * VG],
                                    start=False, stop=(i == C - 1))
                            return r
                        w = [vwv[dg], (sMisc, v_b), (sDV, ln_rdy),
                             (sGP, ln_rdy_gp)]
                        if k >= 4:
                            w.append((sAC, dv_done[k - 4]))
                        sched.add("tensor", vmm, waits=w, incs=[(sPE, 1)])
                        pe_last[("v", dg)] = sPE.bump(1)

                        wdv = [(sPE, pe_last[("v", dg)])]
                        if k >= 4:
                            wdv.append(st_done[k - 4])
                        sched.add("scalar",
                                  (lambda e, pp=pp, pq=pq:
                                   e.activation(pout[pq][:, 0:VG], ps[pp][:, 0:VG],
                                                AF.Copy)),
                                  waits=wdv, incs=[(sAC, 1)])
                        dv_done[k] = sAC.bump(1)
                    if ssg + 4 >= NSS and dg + 2 < NVG:
                        vwv[dg + 2] = emit_vload(dg + 2)
                    for ss in range(ssg, min(ssg + 4, NSS)):
                        k = base + dg * NSS + ss
                        pq = k % 4
                        sched.add("sync",
                                  (lambda e, dg=dg, ss=ss, pq=pq:
                                   e.dma_start(
                                       out=v_tm.ap()[ss * 128:(ss + 1) * 128,
                                                     dg * VG:(dg + 1) * VG],
                                       in_=pout[pq][:, 0:VG])),
                                  waits=[(sAC, dv_done[k])], incs=[(sQs[pq], 16)])
                        st_done[k] = (sQs[pq], sQs[pq].bump(16))
                kidx[0] = base + NVG * NSS

                def drain_s(e):
                    for s in sQs:
                        if s.n:
                            e.wait_ge(s.h, s.n)
                    return None
                sched.add("sync", drain_s)
                run_block(sched, "g2")

        if "g2" in PH:
            g2_phase()

        def g3_phase():
            sched = Sched()
            with ExitStack() as ph:
                NGR = TB // 64
                qblk = [ph.enter_context(nc.sbuf_tensor(f"g3_qb{j}", [128, 32 * TB],
                                                        BF16)) for j in range(2)]
                kblk = [ph.enter_context(nc.sbuf_tensor(f"g3_kb{j}", [128, 32 * TB],
                                                        BF16)) for j in range(2)]
                vt4 = [ph.enter_context(nc.sbuf_tensor(f"g3_vt{j}", [128, 32 * TB],
                                                       BF16)) for j in range(2)]
                p2bd = [ph.enter_context(nc.sbuf_tensor(f"g3_p2{j}", [128, 2048],
                                                        BF16)) for j in range(4)]
                rsT = ph.enter_context(nc.sbuf_tensor("g3_rsT", [128, 32], F32))
                rsThi = ph.enter_context(nc.sbuf_tensor("g3_rsh", [128, 32], BF16))
                rsTlo = ph.enter_context(nc.sbuf_tensor("g3_rsl", [128, 32], BF16))
                rzT_sb = ph.enter_context(nc.sbuf_tensor("g3_rzT", [16, 512], BF16))
                rz_row = [ph.enter_context(nc.sbuf_tensor(f"g3_rzr{j}", [1, 4096],
                                                          BF16)) for j in range(2)]
                ctxe = [ph.enter_context(nc.sbuf_tensor(f"g3_ce{j}", [128, 1024],
                                                        BF16)) for j in range(4)]
                ctxn = ph.enter_context(nc.sbuf_tensor("g3_cn", [128, 32 * TB], BF16))
                y1t = [ph.enter_context(nc.sbuf_tensor(f"g3_y1{j}", [128, TB], F32))
                       for j in range(4)]
                y2t = [ph.enter_context(nc.sbuf_tensor(f"g3_y2{j}", [128, TB], F32))
                       for j in range(4)]
                g_ps = [ph.enter_context(nc.psum_tensor(f"g3_g{j}", [128, PSW], F32))
                        for j in range(2)]
                zT_ps = ph.enter_context(nc.psum_tensor("g3_zT", [128, 32], F32))
                rzrow_ps = ph.enter_context(nc.psum_tensor("g3_rr", [16, 512], F32))
                c_ps = ph.enter_context(nc.psum_tensor("g3_c", [128, 1024], F32))
                rbq_ps = [ph.enter_context(nc.psum_tensor(f"g3_rb{j}", [128, 512],
                                                          F32)) for j in range(2)]

                for j in range(4):
                    sched.add("vector",
                              (lambda e, j=j: e.memset(p2bd[j][:, :], 0.0)),
                              incs=[(sDV, 1)])
                    sDV.bump(1)
                v_ms = sDV.n

                exp_done = {}
                zrec = {}
                zt_done = {}
                tr_done = {}
                ev_done = {}
                d2_done = {}
                rbmm = {}
                rbmm_all = {}
                mult_done = {}
                copy_done = {}
                ctx_done = {}
                gram_done = {}
                pe_blk_done = {}
                y2_st = {}; y2_dv = {}

                def emit_trdma(lin2):
                    gp2 = lin2 % 2
                    w = [(sDV, zrec[lin2])]
                    if lin2 - 2 in ev_done:
                        w.append((sAC, ev_done[lin2 - 2]))

                    def trmm(e, gp2=gp2):
                        e.matmul(rzrow_ps[:, gp2 * 256:gp2 * 256 + 128],
                                 rsThi[:, gp2 * 16:(gp2 + 1) * 16],
                                 identsb[:, :], start=True, stop=True)
                        return e.matmul(
                            rzrow_ps[:, gp2 * 256 + 128:gp2 * 256 + 256],
                            rsTlo[:, gp2 * 16:(gp2 + 1) * 16],
                            identsb[:, :], start=True, stop=True)
                    sched.add("tensor", trmm, waits=w, incs=[(sPE, 1)])
                    tr_done[lin2] = sPE.bump(1)
                    w = [(sPE, tr_done[lin2])]
                    if lin2 - 2 in d2_done:
                        w.append(d2_done[lin2 - 2])
                    sched.add("scalar",
                              (lambda e, gp2=gp2:
                               e.activation(rzT_sb[:, gp2 * 256:(gp2 + 1) * 256],
                                            rzrow_ps[:, gp2 * 256:(gp2 + 1) * 256],
                                            AF.Copy)),
                              waits=w, incs=[(sAC, 1)])
                    ev_done[lin2] = ev_ac = sAC.bump(1)
                    sd2 = sQs[2 + gp2]
                    w = [(sAC, ev_ac)]
                    if lin2 - 2 in rbmm_all:
                        w.append((sPE, rbmm_all[lin2 - 2]))
                    def rowdma(e, gp2=gp2):
                        e.dma_start(
                            out=rz_row[gp2][0:1, 0:2048].rearrange(
                                "o (j p) -> o j p", p=128),
                            in_=rzT_sb[:, gp2 * 256:gp2 * 256 + 128]).then_inc(
                                sd2.h, 16)
                        return e.dma_start(
                            out=rz_row[gp2][0:1, 2048:4096].rearrange(
                                "o (j p) -> o j p", p=128),
                            in_=rzT_sb[:, gp2 * 256 + 128:gp2 * 256 + 256])
                    sched.add("sync", rowdma, waits=w, incs=[(sd2, 16)])
                    sd2.bump(16)
                    d2_done[lin2] = (sd2, sd2.bump(16))

                def emit_rbmult(lin2):
                    g2v = lin2 % NGR
                    gp2 = lin2 % 2
                    for q in range(4):
                        qn = lin2 * 4 + q
                        hn = lin2 * 2 + q // 2
                        w_rb = [d2_done[lin2]]
                        if qn - 2 in mult_done:
                            w_rb.append((sDV, mult_done[qn - 2]))
                        def rbmm_op(e, gp2=gp2, q=q, qn=qn):
                            e.matmul(rbq_ps[qn % 2][:, :],
                                     onesrow[0:1, 0:128],
                                     rz_row[gp2][0:1, q * 512:(q + 1) * 512],
                                     start=True, stop=False)
                            return e.matmul(
                                rbq_ps[qn % 2][:, :],
                                onesrow[0:1, 0:128],
                                rz_row[gp2][0:1, 2048 + q * 512:
                                            2048 + (q + 1) * 512],
                                start=False, stop=True)
                        sched.add("tensor", rbmm_op,
                                  waits=w_rb, incs=[(sPE, 1)])
                        rbmm[qn] = sPE.bump(1)
                        w_ml = [(sPE, rbmm[qn]), (sAC, copy_done[hn])]
                        if g2v == 0 and q == 0 and lin2 >= NGR:
                            w_ml.append((sDV,
                                         y2_dv[(lin2 // NGR - 1) * NH + NH - 1]))
                        sched.add("vector",
                                  (lambda e, g2v=g2v, q=q, qn=qn, hs=hn % 4,
                                   q2=q % 2:
                                   e.tensor_mul(
                                       ctxn[:, g2v * 2048 + q * 512:
                                            g2v * 2048 + (q + 1) * 512],
                                       ctxe[hs][:, q2 * 512:(q2 + 1) * 512],
                                       rbq_ps[qn % 2][:, :])),
                                  waits=w_ml, incs=[(sDV, 1)])
                        mult_done[qn] = sDV.bump(1)
                    rbmm_all[lin2] = rbmm[lin2 * 4 + 3]

                ld_blks = {}

                def emit_block_loads(tb):
                    wld = [] if tb < 2 else \
                        [(sPE, gram_done[(tb - 1) * NGR - 1])]
                    for src_d, dstb in ((qTs, qblk[tb % 2]), (kTs, kblk[tb % 2])):
                        sched.add("sync",
                                  (lambda e, s=src_d, d=dstb, tb=tb:
                                   e.dma_start(
                                       out=d[:, :].rearrange("p (h t) -> p h t", t=TB),
                                       in_=bass.AP(s, tb * TB,
                                                   [[TL, 128], [128 * TL, 32],
                                                    [1, TB]]))),
                                  waits=wld, incs=[(sBlk[tb % 2], 16)])
                        wld = []
                        sBlk[tb % 2].bump(16)
                    w_vt = [] if tb < 2 else \
                        [(sPE, ctx_done.get(((tb - 1) * NGR + NGR - 1, 1), 0))]
                    for tq in range(4):
                        sched.add("sync",
                                  (lambda e, tb=tb, tq=tq:
                                   e.dma_start(
                                       out=vt4[tb % 2][tq * 32:(tq + 1) * 32, :]
                                       .rearrange("h (j d) -> h j d", d=128),
                                       in_=v_tm.ap()[tb * TB:(tb + 1) * TB, :]
                                       .rearrange("(j tq) (h d) -> tq h j d",
                                                  tq=4, h=32)[tq])),
                                  waits=w_vt, incs=[(sBlk[tb % 2], 16)])
                        w_vt = []
                        sBlk[tb % 2].bump(16)
                    ld_blks[tb] = (sBlk[tb % 2], sBlk[tb % 2].n)

                emit_block_loads(0)
                for tb in range(NB):
                    ld_blk = ld_blks[tb]
                    for g in range(NGR):
                        gb = g % 2
                        pb = g % 4
                        lin = tb * NGR + g

                        def gram(e, tb=tb, g=g, gb=gb):
                            r = None
                            ka = kblk[tb % 2][:, :].rearrange(
                                "p (h t) -> p t h", t=TB)
                            qa = qblk[tb % 2][:, :].rearrange(
                                "p (h t) -> p t h", t=TB)
                            for tt in range(64):
                                band = tt % 4
                                tk = g * 64 + tt
                                r = e.matmul(
                                    g_ps[gb][band * 32:(band + 1) * 32,
                                             (tt // 4) * 32:(tt // 4 + 1) * 32],
                                    ka[:, tk, :], qa[:, tk, :],
                                    start=True, stop=True,
                                    tile_position=(0, band * 32))
                            return r
                        w = [ld_blks[tb]]
                        if lin >= 2:
                            w.append((sAC, exp_done[lin - 2]))
                        sched.add("tensor", gram, waits=w, incs=[(sPE, 1)])
                        gram_done[lin] = sPE.bump(1)

                        wexp = [(sPE, gram_done[lin]), (sDV, v_ms)]
                        if lin >= 4:
                            wexp.append((sPE, ctx_done[(lin - 4, 1)]))
                        for band in range(4):
                            in_ap = bass.AP(g_ps[gb], band * 32 * PSW,
                                            [[PSW, 32], [32, 16], [1, 32]])
                            out_ap = bass.AP(p2bd[pb], band * 32 * 2048 + band * 32,
                                             [[2048, 32], [128, 16], [1, 32]])
                            sched.add("scalar",
                                      (lambda e, i=in_ap, o=out_ap:
                                       e.activation(o, i, AF.Exp, scale=sc_mix)),
                                      waits=wexp, incs=[(sAC, 1)])
                            sAC.bump(1)
                        exp_done[lin] = sAC.n

                    for g in range(NGR):
                        pb = g % 4
                        lin = tb * NGR + g
                        gpar = lin % 2

                        w = [(sAC, exp_done[lin])]
                        if lin - 2 in zrec:
                            w.append((sDV, zrec[lin - 2]))

                        def ztmm(e, pb=pb, gpar=gpar):
                            r = None
                            for j in range(16):
                                r = e.matmul(
                                    zT_ps[:, gpar * 16 + j:gpar * 16 + j + 1],
                                    p2bd[pb][:, j * 128:(j + 1) * 128],
                                    onescol[:, 0:1], start=True, stop=True)
                            return r
                        sched.add("tensor", ztmm, waits=w, incs=[(sPE, 1)])
                        zt_done[lin] = sPE.bump(1)

                        w = [(sPE, zt_done[lin])]
                        if lin - 2 in tr_done:
                            w.append((sPE, tr_done[lin - 2]))
                        sl = slice(gpar * 16, (gpar + 1) * 16)
                        zrec[lin] = addseq(sched, "vector", sDV, [
                            (lambda e, sl=sl:
                             e.reciprocal(rsT[:, sl], zT_ps[:, sl])),
                            (lambda e, sl=sl:
                             e.tensor_copy(rsThi[:, sl], rsT[:, sl])),
                            (lambda e, sl=sl:
                             e.tensor_sub(rsTlo[:, sl], rsT[:, sl],
                                          rsThi[:, sl])),
                        ], waits=w)

                        if g >= 1:
                            emit_trdma(lin - 1)

                        for half in range(2):
                            hn = lin * 2 + half
                            w = [(sAC, exp_done[lin]), ld_blk]
                            if hn - 1 in copy_done:
                                w.append((sAC, copy_done[hn - 1]))

                            def ctxmm(e, tb=tb, pb=pb, g=g, half=half):
                                r = None
                                vt = vt4[tb % 2]
                                for j in range(8):
                                    jj = half * 8 + j
                                    vcol = (g * 16 + jj) * 128
                                    r = e.matmul(
                                        c_ps[:, j * 128:(j + 1) * 128],
                                        vt[:, vcol:vcol + 128],
                                        p2bd[pb][:, jj * 128:(jj + 1) * 128],
                                        start=True, stop=True)
                                return r
                            sched.add("tensor", ctxmm, waits=w, incs=[(sPE, 1)])
                            ctx_done[(lin, half)] = sPE.bump(1)

                            w = [(sPE, ctx_done[(lin, half)])]
                            if 2 * hn - 7 in mult_done:
                                w.append((sDV, mult_done[2 * hn - 7]))
                            sched.add("scalar",
                                      (lambda e, hs=hn % 4:
                                       e.activation(ctxe[hs][:, :], c_ps[:, 0:1024],
                                                    AF.Copy)),
                                      waits=w, incs=[(sAC, 1)])
                            copy_done[hn] = sAC.bump(1)

                        if g >= 1:
                            emit_rbmult(lin - 1)
                    emit_trdma(tb * NGR + NGR - 1)
                    emit_rbmult(tb * NGR + NGR - 1)
                    pe_blk_done[tb] = sPE.n
                    if tb + 1 < NB:
                        emit_block_loads(tb + 1)

                    v_norm_all = max(mult_done.values())
                    for h in range(NH):
                        kq = tb * NH + h
                        pq = kq % 4
                        wy = [] if kq < 4 else [(sDV, y2_dv[kq - 4])]
                        sched.add("scalar",
                                  (lambda e, h=h, tb=tb, pq=pq:
                                   e.dma_start(out=y1t[pq][:, :],
                                               in_=y1T.ap()[h * 128:(h + 1) * 128,
                                                            tb * TB:(tb + 1) * TB])),
                                  waits=wy, incs=[(sY1[pq], 16)])
                        v_y1 = (sY1[pq], sY1[pq].bump(16))

                        w = [v_y1, (sDV, v_norm_all)]
                        if kq >= 4:
                            w.append(y2_st[kq - 4])
                        ctx_h = ctxn[:, :].rearrange("p (t h) -> p h t",
                                                     h=32)[:, h:h + 1, :].squeeze(1)
                        sched.add("vector",
                                  (lambda e, pq=pq, ctx_h=ctx_h:
                                   e.tensor_add(y2t[pq][:, :], y1t[pq][:, :], ctx_h)),
                                  waits=w, incs=[(sDV, 1)])
                        y2_dv[kq] = sDV.bump(1)

                        sched.add("gpsimd",
                                  (lambda e, h=h, tb=tb, pq=pq:
                                   e.dma_start(out=y2T.ap()[h * 128:(h + 1) * 128,
                                                            tb * TB:(tb + 1) * TB],
                                               in_=y2t[pq][:, :])),
                                  waits=[(sDV, y2_dv[kq])], incs=[(sY2[pq], 16)])
                        y2_st[kq] = (sY2[pq], sY2[pq].bump(16))

                def drain_s(e):
                    for s in sY2:
                        if s.n:
                            e.wait_ge(s.h, s.n)
                    return None
                sched.add("gpsimd", drain_s)
                run_block(sched, "g3")

        if "g3" in PH:
            g3_phase()

        if "g4" in PH:
            ffn_app(y2T, y2T, OUT, "g4")

        sF = Sched()
        sF.add("sync", lambda e: e.nop() and None)
        run_block(sF, "fin")

    return nc


_CFG = dict(D=4096, TL=2048, TC=512, TB=256)
_CACHE = {}


def _seg_consts():
    seg = np.zeros((128, 16), dtype=np.float32)
    for oc in range(2):
        for p in range(128):
            seg[p, oc * 8 + (p // 32 + 4 * oc)] = 1.0
    seg16 = np.zeros((8, 128), dtype=np.float32)
    for e in range(8):
        seg16[e, e * 16:(e + 1) * 16] = 1.0
    return seg.astype(BF), seg16.astype(BF)


def make_inputs(cfg, w, xs):
    D = cfg["D"]; TL = cfg["TL"]
    E, R = cfg.get("E", 8), cfg.get("R", 16)
    seg, seg16 = _seg_consts()
    shared = {
        "WT": np.ascontiguousarray(w["W"].T.astype(BF)),
        "WqT": np.ascontiguousarray(w["Wq"].T.astype(BF)),
        "WkT": np.ascontiguousarray(w["Wk"].T.astype(BF)),
        "AT": np.ascontiguousarray(w["A"].reshape(E * R, D).T.astype(BF)),
        "BmT": np.ascontiguousarray(
            w["Bm"].transpose(0, 2, 1).reshape(E * R, D).astype(BF)),
        "b_f": np.ascontiguousarray(w["b"].reshape(1, D).astype(BF)),
        "wqT": np.ascontiguousarray(w["wq"].T.astype(BF)),
        "wkT": np.ascontiguousarray(w["wk"].T.astype(BF)),
        "wvT": np.ascontiguousarray(w["wv"].T.astype(BF)),
        "bq_f": np.ascontiguousarray(w["bq"].reshape(1, D).astype(BF)),
        "bk_f": np.ascontiguousarray(w["bk"].reshape(1, D).astype(BF)),
        "bv_f": np.ascontiguousarray(w["bv"].reshape(1, D).astype(BF)),
        "onescol": np.ones((128, 1), dtype=BF),
        "onesrow": np.ones((1, 512), dtype=BF),
        "ones128f": np.ones((1, 128), dtype=np.float32),
        "seg": seg,
        "seg16": seg16,
        "ident": np.eye(128, dtype=BF),
        "identf": np.eye(128, dtype=np.float32),
    }
    in_maps = []
    for c in range(8):
        b, hf = c // 2, c % 2
        m = dict(shared)
        m["xT"] = np.ascontiguousarray(xs[b][hf * TL:(hf + 1) * TL, :].T)
        in_maps.append(m)
    return in_maps


def kernel(L0_attn_hidden_states, L0_attn_wq, L0_attn_bq, L0_attn_wk, L0_attn_bk,
           L0_attn_wv, L0_attn_bv, L0_attn_wl, L0_attn_bl, L0_ffn_W, L0_ffn_b,
           L0_ffn_Wq, L0_ffn_Wk, L0_ffn_A, L0_ffn_Bm):
    D = _CFG["D"]; TL = _CFG["TL"]
    x = np.asarray(L0_attn_hidden_states, dtype=np.float32)
    w = {
        "W": np.asarray(L0_ffn_W, dtype=np.float32),
        "b": np.asarray(L0_ffn_b, dtype=np.float32),
        "Wq": np.asarray(L0_ffn_Wq, dtype=np.float32),
        "Wk": np.asarray(L0_ffn_Wk, dtype=np.float32),
        "A": np.asarray(L0_ffn_A, dtype=np.float32),
        "Bm": np.asarray(L0_ffn_Bm, dtype=np.float32),
        "wq": np.asarray(L0_attn_wq, dtype=np.float32),
        "bq": np.asarray(L0_attn_bq, dtype=np.float32),
        "wk": np.asarray(L0_attn_wk, dtype=np.float32),
        "bk": np.asarray(L0_attn_bk, dtype=np.float32),
        "wv": np.asarray(L0_attn_wv, dtype=np.float32),
        "bv": np.asarray(L0_attn_bv, dtype=np.float32),
    }
    xs = [x[bb] for bb in range(4)]
    in_maps = make_inputs(_CFG, w, xs)

    if "nc" not in _CACHE:
        _CACHE["nc"] = build_mk(_CFG)
    res = run_bass_kernel_spmd(_CACHE["nc"], in_maps, core_ids=list(range(8)))

    out = np.empty((4, 2 * TL, D), dtype=np.float32)
    for c in range(8):
        bb, hf = c // 2, c % 2
        out[bb, hf * TL:(hf + 1) * TL, :] = np.asarray(res.results[c]["OUT"]).T
    return out
